# version 01
# baseline (speedup 1.0000x reference)
import math
from contextlib import ExitStack

import numpy as np
import concourse.bass as bass
import concourse.mybir as mybir
from concourse.bass_utils import run_bass_kernel_spmd

F32 = mybir.dt.float32
BF16 = mybir.dt.bfloat16
AF = mybir.ActivationFunctionType
ALU = mybir.AluOpType
AX = mybir.AxisListType

T = 4096
D = 1024
LN_EPS = 1e-5
RMS_EPS2 = float(np.finfo(np.float32).eps)


class Sems:

    def __init__(self, nc, stack, names):
        self.h = {n: stack.enter_context(nc.semaphore(n)) for n in names}
        self.c = {n: 0 for n in names}

    def inc(self, name, instr, k=1):
        instr.then_inc(self.h[name], k)
        self.c[name] += k

    def base(self, *names):
        return {n: self.c[n] for n in names}

    def flush(self, eng, name, ins):
        ins.then_inc(self.h[name], 1)
        self.c[name] += 1
        eng.wait_ge(self.h[name], self.c[name])

    def xinc(self, eng, flname, ins, milestone, k=1):
        self.flush(eng, flname, ins)
        self.inc(milestone, eng.nop(), k)


_UID = [0]


def _u(name):
    _UID[0] += 1
    return f"{name}_{_UID[0]}"


def ap3(t, off, d0, d1, d2):
    return bass.AP(t, off, [list(d0), list(d1), list(d2)])


def xslice_ap(x_dram, tok, twid, ncols=T):
    return ap3(x_dram, tok, (ncols, 128), (128 * ncols, 8), (1, twid))



def emit_ln(nc, S, P, xin_dram, xn, m, n_t512, rms=None):
    ones1 = P["ones1"]
    t0 = m * n_t512 * 512
    r2 = 2 if rms is not None else 1
    b = S.base("lnx", "lnsq", "lnmm", "lnst", "lnv1", "lns1", "lnab", "lnk1",
               "lnbc", "lnap")

    with ExitStack() as ps:
      with nc.Block() as blk:
        sb = lambda n, sh, dt: ps.enter_context(nc.sbuf_tensor(_u(n), sh, dt))
        xf = sb("ln_xf", [128, 8, 512], F32)
        sq = sb("ln_sq", [128, 8, 512], F32)
        abbc = sb("ln_abbc", [128, 2, 512], F32)
        R = lambda k: sq[0:1, k, :]
        p_sum = ps.enter_context(nc.psum_tensor(_u("p_sum"), [1, 512], F32))
        p_sq = ps.enter_context(nc.psum_tensor(_u("p_sq"), [1, 512], F32))
        p_a = ps.enter_context(nc.psum_tensor(_u("p_a"), [128, 512], F32))
        p_b = ps.enter_context(nc.psum_tensor(_u("p_b"), [128, 512], F32))

        @blk.sync
        def _(sy):
            for q in range(n_t512):
                if q >= 1:
                    sy.wait_ge(S.h["lnap"], b["lnap"] + q)
                S.inc("lnx", sy.dma_start(out=xf[:, :, :],
                                          in_=xslice_ap(xin_dram, t0 + q * 512, 512)), 16)

        @blk.scalar
        def _(s):
            for q in range(n_t512):
                s.wait_ge(S.h["lnx"], b["lnx"] + 16 * (q + 1))
                if q >= 1:
                    s.wait_ge(S.h["lnap"], b["lnap"] + q)
                ins = None
                for dc in range(8):
                    ins = s.activation(out=sq[:, dc, :], in_=xf[:, dc, :], func=AF.Square)
                S.inc("lnsq", ins, 1)
                s.wait_ge(S.h["lnv1"], b["lnv1"] + q + 1)
                ins = s.activation(out=R(3), in_=R(2), func=AF.Sqrt)
                S.inc("lns1", ins, 1)
                if rms is not None:
                    s.wait_ge(S.h["lnab"], b["lnab"] + r2 * q + 1)
                    ins = s.activation(out=R(7), in_=R(6), func=AF.Sqrt)
                    S.inc("lns1", ins, 1)

        @blk.tensor
        def _(t):
            for q in range(n_t512):
                t.wait_ge(S.h["lnsq"], b["lnsq"] + q + 1)
                if b["lnst"] + q >= 1:
                    t.wait_ge(S.h["lnst"], b["lnst"] + q)
                ins = None
                for dc in range(8):
                    t.matmul(p_sum[:, :], ones1[0:128, 0:1], xf[:, dc, :],
                             start=(dc == 0), stop=(dc == 7))
                    ins = t.matmul(p_sq[:, :], ones1[0:128, 0:1], sq[:, dc, :],
                                   start=(dc == 0), stop=(dc == 7))
                S.inc("lnmm", ins, 1)
                t.wait_ge(S.h["lnab"], b["lnab"] + r2 * (q + 1))
                if b["lnbc"] + 2 * q >= 2:
                    t.wait_ge(S.h["lnbc"], b["lnbc"] + 2 * q)
                ins = t.matmul(p_a[:, :], ones1[0:1, 0:128], R(4),
                               start=True, stop=True)
                S.inc("lnk1", ins, 1)
                ins = t.matmul(p_b[:, :], ones1[0:1, 0:128], R(5),
                               start=True, stop=True)
                S.inc("lnk1", ins, 1)

        @blk.vector
        def _(v):
            FL = lambda ins: S.flush(v, "vfl", ins)
            for q in range(n_t512):
                v.wait_ge(S.h["lnmm"], b["lnmm"] + q + 1)
                v.tensor_single_scalar(R(0), p_sum[:, :], 1.0 / D, ALU.mult)
                ins = v.tensor_single_scalar(R(1), p_sq[:, :], 1.0 / D, ALU.mult)
                S.inc("lnst", ins, 1)
                v.tensor_tensor(R(2), R(0), R(0), ALU.mult)
                v.tensor_tensor(R(1), R(1), R(2), ALU.subtract)
                ins = v.tensor_single_scalar(R(2), R(1), LN_EPS, ALU.add)
                S.inc("lnv1", ins, 1)
                v.wait_ge(S.h["lns1"], b["lns1"] + r2 * q + 1)
                v.reciprocal(R(4), R(3))
                if rms is None:
                    v.tensor_tensor(R(5), R(0), R(4), ALU.mult)
                    ins = v.tensor_single_scalar(R(5), R(5), -1.0, ALU.mult)
                    S.inc("lnab", ins, 1)
                else:
                    v.tensor_tensor(R(6), R(1), R(4), ALU.mult)
                    v.tensor_tensor(R(6), R(6), R(4), ALU.mult)
                    ins = v.tensor_single_scalar(R(6), R(6), RMS_EPS2, ALU.add)
                    S.inc("lnab", ins, 1)
                    v.wait_ge(S.h["lns1"], b["lns1"] + 2 * (q + 1))
                    v.reciprocal(R(3), R(7))
                    v.tensor_tensor(R(4), R(4), R(3), ALU.mult)
                    v.tensor_tensor(R(5), R(0), R(4), ALU.mult)
                    ins = v.tensor_single_scalar(R(5), R(5), -1.0, ALU.mult)
                    S.inc("lnab", ins, 1)
                v.wait_ge(S.h["lnk1"], b["lnk1"] + 2 * (q + 1))
                ins = v.tensor_copy(abbc[:, 0, :], p_a[:, :])
                S.inc("lnbc", ins, 1)
                ins = v.tensor_copy(abbc[:, 1, :], p_b[:, :])
                S.inc("lnbc", ins, 1)
                ins = None
                for dc in range(8):
                    v.tensor_tensor(sq[:, dc, :], xf[:, dc, :], abbc[:, 0, :], ALU.mult)
                    if rms is None:
                        ins = v.tensor_tensor(xn[:, dc, q * 512:(q + 1) * 512],
                                              sq[:, dc, :], abbc[:, 1, :], ALU.add)
                    else:
                        v.tensor_tensor(sq[:, dc, :], sq[:, dc, :], abbc[:, 1, :], ALU.add)
                        ins = v.tensor_single_scalar(xn[:, dc, q * 512:(q + 1) * 512],
                                                     sq[:, dc, :],
                                                     rms["nw_cols"][dc][:, :], ALU.mult)
                S.inc("lnap", ins, 1)



def emit_gates(nc, S, P, fam, xn, m, n_t512, dbg=None):
    w8, wg_sb, idf = P["w8"], P["wg_sb"], P["idf"]
    nC = n_t512 * 4
    soft = fam["gate"] == "soft"
    b = S.base("gmm", "gsg", "gw", "gtr", "gtw", "gvm")

    with ExitStack() as ps:
      with nc.Block() as blk:
        sb = lambda n, sh, dt: ps.enter_context(nc.sbuf_tensor(_u(n), sh, dt))
        gtok = sb("g_gtok", [128, 8], F32)
        wtok = sb("g_wtok", [128, 8], F32)
        msk = sb("g_msk", [128, 8], F32)
        zer = sb("g_zer", [128, 8], F32)
        c_m1 = sb("g_m1", [128, 1], F32)
        c_m2 = sb("g_m2", [128, 1], F32)
        c_den = sb("g_den", [128, 1], F32)
        c_rcp = sb("g_rcp", [128, 1], F32)
        c_neg = sb("g_neg", [128, 1], F32)
        p_g = ps.enter_context(nc.psum_tensor(_u("p_g"), [128, 8], F32))
        p_wt = ps.enter_context(nc.psum_tensor(_u("p_wt"), [8, 128], F32))

        @blk.tensor
        def _(t):
            for c in range(nC):
                if b["gsg"] + c >= 1:
                    t.wait_ge(S.h["gsg"], b["gsg"] + c)
                ins = None
                for K in range(8):
                    ins = t.matmul(p_g[:, :], xn[:, K, c * 128:(c + 1) * 128],
                                   wg_sb[:, K, :], start=(K == 0), stop=(K == 7))
                S.inc("gmm", ins, 1)
                t.wait_ge(S.h["gw"], b["gw"] + c + 1)
                if b["gtw"] + c >= 1:
                    t.wait_ge(S.h["gtw"], b["gtw"] + c)
                ins = t.transpose(p_wt[:, :], wtok[:, :], idf[:, :])
                S.inc("gtr", ins, 1)

        @blk.scalar
        def _(s):
            for c in range(nC):
                if b["gw"] + c >= 1:
                    s.wait_ge(S.h["gw"], b["gw"] + c)
                if soft:
                    s.wait_ge(S.h["gvm"], b["gvm"] + c + 1)
                    ins = s.activation(out=gtok[:, :], in_=p_g[:, :], func=AF.Exp,
                                       bias=c_neg[:, :], scale=1.0)
                else:
                    s.wait_ge(S.h["gmm"], b["gmm"] + c + 1)
                    ins = s.activation(out=gtok[:, :], in_=p_g[:, :], func=AF.Sigmoid)
                S.inc("gsg", ins, 1)

        loc = dict(gtok=gtok, wtok=wtok, c_m1=c_m1, c_m2=c_m2,
                   c_rcp=c_rcp, c_den=c_den, zer=zer)

        @blk.vector
        def _(v):
            v.memset(zer[:, :], 0.0)
            for c in range(nC):
                if soft:
                    v.wait_ge(S.h["gmm"], b["gmm"] + c + 1)
                    S.flush(v, "vfl", v.reduce_max(c_m1[:, :], p_g[:, :], AX.X))
                    ins = v.tensor_single_scalar(c_neg[:, :], c_m1[:, :], -1.0, ALU.mult)
                    S.inc("gvm", ins, 1)
                v.wait_ge(S.h["gsg"], b["gsg"] + c + 1)
                if b["gtr"] + c >= 1:
                    v.wait_ge(S.h["gtr"], b["gtr"] + c)
                FL = lambda ins: S.flush(v, "vfl", ins)
                if soft:
                    FL(v.reduce_sum(c_den[:, :], gtok[:, :], AX.X))
                    FL(v.reciprocal(c_rcp[:, :], c_den[:, :]))
                    FL(v.tensor_tensor(gtok[:, :], gtok[:, :],
                                       c_rcp[:, 0:1].broadcast_to((128, 8)), ALU.mult))
                FL(v.reduce_max(c_m1[:, :], gtok[:, :], AX.X))
                FL(v.tensor_tensor(msk[:, :], gtok[:, :],
                                   c_m1[:, 0:1].broadcast_to((128, 8)), ALU.is_lt))
                FL(v.tensor_tensor(wtok[:, :], gtok[:, :], msk[:, :], ALU.mult))
                FL(v.reduce_max(c_m2[:, :], wtok[:, :], AX.X))
                FL(v.tensor_tensor(msk[:, :], gtok[:, :],
                                   c_m2[:, 0:1].broadcast_to((128, 8)), ALU.is_ge))
                FL(v.tensor_tensor(wtok[:, :], gtok[:, :], msk[:, :], ALU.mult))
                if not soft:
                    FL(v.tensor_tensor(c_den[:, :], c_m1[:, :], c_m2[:, :], ALU.add))
                    FL(v.reciprocal(c_rcp[:, :], c_den[:, :]))
                    FL(v.tensor_tensor(wtok[:, :], wtok[:, :],
                                       c_rcp[:, 0:1].broadcast_to((128, 8)), ALU.mult))
                ins = v.tensor_copy(msk[:, :], wtok[:, :])
                S.inc("gw", ins, 1)
                v.wait_ge(S.h["gtr"], b["gtr"] + c + 1)
                ins = v.tensor_copy(w8[0:8, c * 128:(c + 1) * 128], p_wt[0:8, :])
                S.inc("gtw", ins, 1)

      if dbg is not None:
        for k, t in loc.items():
            _dbg_dump(nc, S, t, dbg[k])



def emit_experts(nc, S, P, fam, xn, m, n_t512):
    w8, sel_sb, accf, ones1, onesbc = P["w8"], P["sel_sb"], P["accf"], P["ones1"], P["onesbc"]
    nT = n_t512
    units = fam["units"]
    granules = fam["granules"]

    b = S.base("dw", "wbck1", "wbcev", "h1g", "h3g", "hsl", "h3w", "hmul",
               "dnmm", "accev")
    sched = []
    cnt = dict(dw=0, wbck1=0, wbcev=0, h1g=0, h3g=0, hsl=0, h3w=0, hmul=0,
               dnmm=0, accev=0)
    fi_glob = 0
    for u, unit in enumerate(units):
        for g, (f0, fg) in enumerate(granules):
            nfi = fg // 128
            sched.append(dict(u=u, g=g, f0=f0, fg=fg, nfi=nfi, gi=len(sched),
                              unit=unit, bc=dict(cnt), fi0=fi_glob))
            cnt["dw"] += 48
            if g == 0 and unit["gated"]:
                cnt["wbck1"] += nT
                cnt["wbcev"] += nT
            cnt["h1g"] += 4 * nfi
            cnt["h3g"] += 4 * nfi
            cnt["hsl"] += 4 * nfi
            cnt["h3w"] += 4 * nfi
            cnt["hmul"] += 4 * nfi
            cnt["dnmm"] += nT * 8
            cnt["accev"] += nT * 8
            fi_glob += nfi

    with ExitStack() as ps, nc.Block() as blk:
        sb = lambda n, sh, dt: ps.enter_context(nc.sbuf_tensor(_u(n), sh, dt))
        w1g = [sb("w1g0", [128, 8, 512], BF16), sb("w1g1", [128, 8, 512], BF16)]
        w3g = [sb("w3g0", [128, 8, 512], BF16), sb("w3g1", [128, 8, 512], BF16)]
        w2g = [sb("w2g0", [128, 4, 1024], BF16), sb("w2g1", [128, 4, 1024], BF16)]
        hsb = sb("hsb", [128, 4, 4, 512], BF16)
        silu = sb("silu", [128, 2, 4, 512], BF16)
        wbc = sb("wbc", [128, 4, 512], F32)
        ph1 = ps.enter_context(nc.psum_tensor(_u("ph1"), [128, 4, 512], F32))
        ph3 = ps.enter_context(nc.psum_tensor(_u("ph3"), [128, 4, 512], F32))

        @blk.gpsimd
        def _(gp):
            for ent in sched:
                gi = ent["gi"]
                if gi >= 2:
                    gi2 = sched[gi - 2]
                    gp.wait_ge(S.h["dnmm"], b["dnmm"] + gi2["bc"]["dnmm"] + nT * 8)
                buf = gi % 2
                un = ent["unit"]
                f0, fg, nfi = ent["f0"], ent["fg"], ent["nfi"]
                Fd = un["F"]
                S.inc("dw", gp.dma_start(
                    out=w1g[buf][:, :, 0:fg],
                    in_=ap3(un["w1t"], un["w1off"] + f0, (Fd, 128), (128 * Fd, 8), (1, fg))), 16)
                S.inc("dw", gp.dma_start(
                    out=w3g[buf][:, :, 0:fg],
                    in_=ap3(un["w3t"], un["w3off"] + f0, (Fd, 128), (128 * Fd, 8), (1, fg))), 16)
                S.inc("dw", gp.dma_start(
                    out=w2g[buf][:, 0:nfi, :],
                    in_=ap3(un["w2t"], un["w2off"] + f0 * D, (D, 128), (128 * D, nfi), (1, D))), 16)

        @blk.tensor
        def _(t):
            for ent in sched:
                gi, u, g, fg, nfi = ent["gi"], ent["u"], ent["g"], ent["fg"], ent["nfi"]
                bc = ent["bc"]
                buf = gi % 2
                gated = ent["unit"]["gated"]
                t.wait_ge(S.h["dw"], b["dw"] + bc["dw"] + 48)
                if g == 0 and gated:
                    for tt in range(nT):
                        if bc["wbcev"] + tt >= 1:
                            t.wait_ge(S.h["wbcev"], b["wbcev"] + bc["wbcev"] + tt)
                        if bc["h3w"] >= 1:
                            t.wait_ge(S.h["h3w"], b["h3w"] + bc["h3w"])
                        ins = t.matmul(ph3[:, 0, :], sel_sb[0:8, u, :],
                                       w8[0:8, tt * 512:(tt + 1) * 512],
                                       start=True, stop=True)
                        S.inc("wbck1", ins, 1)
                for fi in range(nfi):
                    fig = ent["fi0"] + fi
                    if b["hsl"] + 4 * fig >= 4:
                        t.wait_ge(S.h["hsl"], b["hsl"] + 4 * fig)
                    if fi == 0 and bc["accev"] >= 1:
                        t.wait_ge(S.h["accev"], b["accev"] + bc["accev"])
                    for K in range(8):
                        for tt in range(nT):
                            ins = t.matmul(ph1[:, tt, :],
                                           w1g[buf][:, K, fi * 128:(fi + 1) * 128],
                                           xn[:, K, tt * 512:(tt + 1) * 512],
                                           start=(K == 0), stop=(K == 7))
                            if K == 7:
                                S.inc("h1g", ins, 1)
                    if b["h3w"] + 4 * fig >= 4:
                        t.wait_ge(S.h["h3w"], b["h3w"] + 4 * fig)
                    if g == 0 and fi == 0 and gated:
                        t.wait_ge(S.h["wbcev"], b["wbcev"] + bc["wbcev"] + nT)
                    for K in range(8):
                        for tt in range(nT):
                            ins = t.matmul(ph3[:, tt, :],
                                           w3g[buf][:, K, fi * 128:(fi + 1) * 128],
                                           xn[:, K, tt * 512:(tt + 1) * 512],
                                           start=(K == 0), stop=(K == 7))
                            if K == 7:
                                S.inc("h3g", ins, 1)
                t.wait_ge(S.h["hsl"], b["hsl"] + bc["hsl"] + 4 * nfi)
                t.wait_ge(S.h["hmul"], b["hmul"] + bc["hmul"] + 4 * nfi)
                for tt in range(nT):
                    for d in range(8):
                        need = bc["dnmm"] + 8 * tt + d - 1
                        if need >= 1:
                            t.wait_ge(S.h["accev"], b["accev"] + need)
                        for fi in range(nfi):
                            ins = t.matmul(ph1[:, d % 2, :],
                                           w2g[buf][:, fi, d * 128:(d + 1) * 128],
                                           hsb[:, fi, tt, :],
                                           start=(fi == 0), stop=(fi == nfi - 1))
                        S.inc("dnmm", ins, 1)

        @blk.scalar
        def _(s):
            for ent in sched:
                for fi in range(ent["nfi"]):
                    fig = ent["fi0"] + fi
                    if fig >= 2:
                        s.wait_ge(S.h["hmul"], b["hmul"] + 4 * (fig - 1))
                    for tt in range(nT):
                        s.wait_ge(S.h["h1g"], b["h1g"] + 4 * fig + tt + 1)
                        ins = s.activation(out=silu[:, fig % 2, tt, :], in_=ph1[:, tt, :],
                                           func=AF.Silu)
                        S.inc("hsl", ins, 1)

        @blk.vector
        def _(v):
            for ent in sched:
                gi, g, nfi = ent["gi"], ent["g"], ent["nfi"]
                bc = ent["bc"]
                gated = ent["unit"]["gated"]
                wsrc = wbc if gated else onesbc
                if g == 0 and gated:
                    for tt in range(nT):
                        v.wait_ge(S.h["wbck1"], b["wbck1"] + bc["wbck1"] + tt + 1)
                        ins = v.tensor_copy(wbc[:, tt, :], ph3[:, 0, :])
                        S.inc("wbcev", ins, 1)
                for fi in range(nfi):
                    fig = ent["fi0"] + fi
                    for tt in range(nT):
                        v.wait_ge(S.h["h3g"], b["h3g"] + 4 * fig + tt + 1)
                        win = wsrc[:, tt, :] if gated else onesbc[:, :]
                        ins = v.scalar_tensor_tensor(hsb[:, fi, tt, :], ph3[:, tt, :], 1.0,
                                                     win, ALU.mult, ALU.mult)
                        S.inc("h3w", ins, 1)
                        v.wait_ge(S.h["hsl"], b["hsl"] + 4 * fig + tt + 1)
                        ins = v.tensor_tensor(hsb[:, fi, tt, :], silu[:, fig % 2, tt, :],
                                              hsb[:, fi, tt, :], ALU.mult)
                        S.inc("hmul", ins, 1)
                for tt in range(nT):
                    for d in range(8):
                        v.wait_ge(S.h["dnmm"], b["dnmm"] + bc["dnmm"] + 8 * tt + d + 1)
                        dst = accf[:, d, tt * 512:(tt + 1) * 512]
                        if ent["u"] == 0 and g == 0:
                            ins = v.tensor_copy(dst, ph1[:, d % 2, :])
                        else:
                            ins = v.scalar_tensor_tensor(dst, ph1[:, d % 2, :], 1.0,
                                                         dst, ALU.mult, ALU.add)
                        S.inc("accev", ins, 1)



def emit_epilogue(nc, S, xin_dram, xout_dram, accf, scale, m, n_t512):
    t0 = m * n_t512 * 512
    b = S.base("epx", "epv", "eps")
    with ExitStack() as ps, nc.Block() as blk:
        sb = lambda n, sh, dt: ps.enter_context(nc.sbuf_tensor(_u(n), sh, dt))
        xf = sb("ep_xf", [128, 8, 512], F32)
        outst = sb("ep_out", [128, 8, 512], F32)

        @blk.sync
        def _(sy):
            for q in range(n_t512):
                if q >= 1:
                    sy.wait_ge(S.h["epv"], b["epv"] + q)
                S.inc("epx", sy.dma_start(out=xf[:, :, :],
                                          in_=xslice_ap(xin_dram, t0 + q * 512, 512)), 16)
                sy.wait_ge(S.h["epv"], b["epv"] + q + 1)
                S.inc("eps", sy.dma_start(
                    out=xslice_ap(xout_dram, t0 + q * 512, 512),
                    in_=outst[:, :, :]), 16)

        @blk.vector
        def _(v):
            for q in range(n_t512):
                v.wait_ge(S.h["epx"], b["epx"] + 16 * (q + 1))
                if q >= 1:
                    v.wait_ge(S.h["eps"], b["eps"] + 16 * q)
                ins = None
                for dc in range(8):
                    ins = v.scalar_tensor_tensor(outst[:, dc, :],
                                                 accf[:, dc, q * 512:(q + 1) * 512], scale,
                                                 xf[:, dc, :], ALU.mult, ALU.add)
                S.inc("epv", ins, 1)



FFN_SEMS = ["lnx", "lnsq", "lnmm", "lnst", "lnv1", "lns1", "lnab", "lnk1", "lnbc",
            "lnap", "gmm", "gsg", "gw", "gtr", "gtw", "gvm", "dg", "dw", "vfl", "wbck1",
            "wbcev", "h1g", "h3g", "hsl", "h3w", "hmul", "dnmm", "accev",
            "epx", "epv", "eps"]


def make_ffn_fam(nc, kind):
    if kind == "moe0":
        Wg = nc.dram_tensor("Wg", [D, 8], F32, kind="ExternalInput")
        Ws1 = nc.dram_tensor("Ws1", [D, 2048], F32, kind="ExternalInput")
        Ws3 = nc.dram_tensor("Ws3", [D, 2048], F32, kind="ExternalInput")
        Ws2 = nc.dram_tensor("Ws2", [2048, D], F32, kind="ExternalInput")
        We1 = nc.dram_tensor("We1", [8, D, 2048], F32, kind="ExternalInput")
        We3 = nc.dram_tensor("We3", [8, D, 2048], F32, kind="ExternalInput")
        We2 = nc.dram_tensor("We2", [8, 2048, D], F32, kind="ExternalInput")
        units = []
        for e in range(8):
            units.append(dict(F=2048, slot=e, gated=True,
                              w1t=We1, w1off=e * D * 2048, w3t=We3, w3off=e * D * 2048,
                              w2t=We2, w2off=e * 2048 * D))
        units.append(dict(F=2048, slot=8, gated=False,
                          w1t=Ws1, w1off=0, w3t=Ws3, w3off=0, w2t=Ws2, w2off=0))
        return dict(kind=kind, gate="sig", scale=1.0 / 6.0, wg=Wg, units=units,
                    granules=[(0, 512), (512, 512), (1024, 512), (1536, 512)])
    else:
        wrT = nc.dram_tensor("wrT", [D, 8], F32, kind="ExternalInput")
        w1 = nc.dram_tensor("w1", [8, D, 1408], F32, kind="ExternalInput")
        w3 = nc.dram_tensor("w3", [8, D, 1408], F32, kind="ExternalInput")
        w2 = nc.dram_tensor("w2", [8, 1408, D], F32, kind="ExternalInput")
        units = []
        for e in range(8):
            units.append(dict(F=1408, slot=e, gated=True,
                              w1t=w1, w1off=e * D * 1408, w3t=w3, w3off=e * D * 1408,
                              w2t=w2, w2off=e * 1408 * D))
        return dict(kind=kind, gate="soft", scale=0.5, wg=wrT, units=units,
                    granules=[(0, 512), (512, 512), (1024, 384)])


def alloc_persistent(nc, st):
    sb = lambda n, sh, dt: st.enter_context(nc.sbuf_tensor(_u(n), sh, dt))
    return dict(
        ones1=sb("ones1", [128, 128], F32),
        idf=sb("idf", [128, 128], F32),
        onesbc=sb("onesbc", [128, 512], F32),
        xn=sb("xn", [128, 8, 2048], BF16),
        accf=sb("accf", [128, 8, 2048], F32),
        w8=sb("w8", [8, 2048], F32),
        sel_sb=sb("sel_sb", [8, 9, 128], F32),
        wg_sb=sb("wg_sb", [128, 8, 8], BF16),
    )


def emit_ffn_init(nc, S, P, fam, ident, sel):
    with nc.Block() as blk:
        @blk.gpsimd
        def _(g):
            g.memset(P["ones1"][:, :], 1.0)
            g.memset(P["onesbc"][:, :], 1.0)
            S.inc("dg", g.dma_start(out=P["idf"][:, :], in_=ident.ap()), 16)
            S.inc("dg", g.dma_start(out=P["sel_sb"][:, :, :], in_=sel.ap()), 16)
            S.inc("dg", g.dma_start(
                out=P["wg_sb"][:, :, :],
                in_=ap3(fam["wg"], 0, (8, 128), (128 * 8, 8), (1, 8))), 16)
            g.wait_ge(S.h["dg"], S.c["dg"])


def build_ffn_nc(kind, debug=False):
    nc = bass.Bass("TRN2", target_bir_lowering=False, detect_race_conditions=False)
    xin = nc.dram_tensor("xin", [D, T], F32, kind="ExternalInput")
    ident = nc.dram_tensor("ident", [128, 128], F32, kind="ExternalInput")
    sel = nc.dram_tensor("sel", [8, 9, 128], F32, kind="ExternalInput")
    xout = nc.dram_tensor("xout", [D, T], F32, kind="ExternalOutput")
    fam = make_ffn_fam(nc, kind)
    if debug:
        dbg_xn = nc.dram_tensor("dbg_xn", [128, 8, 2048], BF16, kind="ExternalOutput")
        dbg_w8 = nc.dram_tensor("dbg_w8", [8, 2048], F32, kind="ExternalOutput")
        dbg_acc = nc.dram_tensor("dbg_acc", [128, 8, 2048], F32, kind="ExternalOutput")
        dbg_g = {k: nc.dram_tensor("dbg_" + k, [128, 8] if k in ("gtok", "wtok", "zer") else [128, 1],
                                   F32, kind="ExternalOutput")
                 for k in ("gtok", "wtok", "c_m1", "c_m2", "c_rcp", "c_den", "zer")}

    with ExitStack() as st:
        S = Sems(nc, st, FFN_SEMS)
        P = alloc_persistent(nc, st)
        emit_ffn_init(nc, S, P, fam, ident, sel)
        for m in range(2):
            emit_ln(nc, S, P, xin, P["xn"], m, 4)
            if debug and m == 0:
                _dbg_dump(nc, S, P["xn"], dbg_xn)
            emit_gates(nc, S, P, fam, P["xn"], m, 4,
                       dbg=(dbg_g if (debug and m == 0) else None))
            if debug and m == 0:
                _dbg_dump(nc, S, P["w8"], dbg_w8)
            emit_experts(nc, S, P, fam, P["xn"], m, 4)
            if debug and m == 0:
                _dbg_dump(nc, S, P["accf"], dbg_acc)
            emit_epilogue(nc, S, xin, xout, P["accf"], fam["scale"], m, 4)
        with nc.Block() as blk:
            @blk.sync
            def _(sy):
                sy.wait_ge(S.h["eps"], S.c["eps"])
    return nc


def _dbg_dump(nc, S, sbuf_t, dram_t):
    with nc.Block() as blk:
        @blk.sync
        def _(sy):
            full = [slice(None)] * len(sbuf_t.shape)
            S.inc("eps", sy.dma_start(out=dram_t.ap(), in_=sbuf_t[tuple(full)]), 16)
            sy.wait_ge(S.h["eps"], S.c["eps"])





ATTN0_SEMS = ["sfl", "aq_mm", "aq_ev", "av1", "as1", "atr_mm", "atr_ev", "a_smm",
              "a_sev", "a_dg", "a_omm", "a_zr", "a_zk", "a_zbev", "a_oev",
              "a_wo"]


def emit_attn0_qkv(nc, S, P, xn, c0=0, nc_chunks=32, dbg=None):
    b = S.base("aq_mm", "aq_ev", "av1", "as1")
    with ExitStack() as ps:
      with nc.Block() as blk:
        sb = lambda n, sh, dt: ps.enter_context(nc.sbuf_tensor(_u(n), sh, dt))
        qd = sb("a_qd", [128, 2, 256], F32)
        sqs = sb("a_sqs", [128, 256], F32)
        m16 = sb("a_m16", [128, 16], F32)
        v16 = sb("a_v16", [128, 16], F32)
        srt = sb("a_srt", [128, 16], F32)
        rsd = sb("a_rsd", [128, 16], F32)
        t1 = sb("a_t1", [128, 256], F32)
        pq = [ps.enter_context(nc.psum_tensor(_u("a_pq"), [128, 256], F32)) for _ in range(2)]
        pk = [ps.enter_context(nc.psum_tensor(_u("a_pk"), [128, 256], F32)) for _ in range(2)]
        pv = [ps.enter_context(nc.psum_tensor(_u("a_pv"), [128, 1024], F32)) for _ in range(2)]

        @blk.tensor
        def _(t):
            for cc in range(nc_chunks):
                pr = cc % 2
                if cc >= 2:
                    t.wait_ge(S.h["aq_ev"], b["aq_ev"] + cc - 1)
                ins = None
                for K in range(8):
                    st_ = (K == 0)
                    sp = (K == 7)
                    xl = xn[:, K, cc * 128:(cc + 1) * 128]
                    t.matmul(pq[pr][:, :], xl, P["wq_sb"][:, K, :], start=st_, stop=sp)
                    t.matmul(pk[pr][:, :], xl, P["wk_sb"][:, K, :], start=st_, stop=sp)
                    t.matmul(pv[pr][:, 0:512], xl, P["wv_sb"][:, K, 0:512], start=st_, stop=sp)
                    ins = t.matmul(pv[pr][:, 512:1024], xl, P["wv_sb"][:, K, 512:1024],
                                   start=st_, stop=sp)
                S.inc("aq_mm", ins, 1)

        @blk.scalar
        def _(s):
            for cc in range(nc_chunks):
                for x2 in range(2):
                    s.wait_ge(S.h["av1"], b["av1"] + 4 * cc + 2 * x2 + 1)
                    ins = s.activation(out=sqs[:, :], in_=qd[:, x2, :], func=AF.Square)
                    S.xinc(s, "sfl", ins, "as1")
                    s.wait_ge(S.h["av1"], b["av1"] + 4 * cc + 2 * x2 + 2)
                    ins = s.activation(out=srt[:, :], in_=v16[:, :], func=AF.Sqrt)
                    S.xinc(s, "sfl", ins, "as1")

        @blk.vector
        def _(v):
            FL = lambda ins: S.flush(v, "vfl", ins)
            for cc in range(nc_chunks):
                c = c0 + cc
                pr = cc % 2
                v.wait_ge(S.h["aq_mm"], b["aq_mm"] + cc + 1)
                for x2 in range(2):
                    pp = pq[pr] if x2 == 0 else pk[pr]
                    FL(v.tensor_reduce(m16[:, :],
                                       pp[:, :].rearrange("p (h f) -> p h f", h=16),
                                       AX.X, ALU.add))
                    FL(v.tensor_single_scalar(m16[:, :], m16[:, :], 1.0 / 16, ALU.mult))
                    ins = v.tensor_tensor(qd[:, x2, :].rearrange("p (h f) -> p h f", h=16),
                                          pp[:, :].rearrange("p (h f) -> p h f", h=16),
                                          m16[:, :].rearrange("p (h o) -> p h o", o=1)
                                          .broadcast_to((128, 16, 16)), ALU.subtract)
                    S.xinc(v, "vfl", ins, "av1")
                    v.wait_ge(S.h["as1"], b["as1"] + 4 * cc + 2 * x2 + 1)
                    FL(v.tensor_reduce(v16[:, :],
                                       sqs[:, :].rearrange("p (h f) -> p h f", h=16),
                                       AX.X, ALU.add))
                    FL(v.tensor_single_scalar(v16[:, :], v16[:, :], 1.0 / 16, ALU.mult))
                    ins = v.tensor_single_scalar(v16[:, :], v16[:, :], LN_EPS, ALU.add)
                    S.xinc(v, "vfl", ins, "av1")
                    v.wait_ge(S.h["as1"], b["as1"] + 4 * cc + 2 * x2 + 2)
                    FL(v.reciprocal(rsd[:, :], srt[:, :]))
                    v.tensor_tensor(t1[:, :].rearrange("p (h f) -> p h f", h=16),
                                    qd[:, x2, :].rearrange("p (h f) -> p h f", h=16),
                                    rsd[:, :].rearrange("p (h o) -> p h o", o=1)
                                    .broadcast_to((128, 16, 16)), ALU.mult)
                    gsrc = P["gq_bc"] if x2 == 0 else P["gk_bc"]
                    bsrc = P["bq_bc"] if x2 == 0 else P["bk_bc"]
                    dtok = P["qtok"] if x2 == 0 else P["ktok"]
                    v.tensor_tensor(t1[:, :], t1[:, :], gsrc[:, :], ALU.mult)
                    v.tensor_tensor(dtok[:, c, :], t1[:, :], bsrc[:, :], ALU.add)
                ins = v.tensor_copy(
                    P["v65"][:, c, :, 0:64],
                    pv[pr][:, :].rearrange("p (h f) -> p h f", h=16))
                S.inc("aq_ev", ins, 1)

      if dbg is not None:
        for nm, tt_ in (("m16", m16), ("v16", v16), ("rsd", rsd), ("srt", srt),
                        ("qd", qd), ("t1", t1)):
            if nm in dbg:
                _dbg_dump(nc, S, tt_, dbg[nm])


def emit_attn0_tr(nc, S, P):
    b = S.base("atr_mm", "atr_ev")
    with ExitStack() as ps:
      with nc.Block() as blk:
        pt = [ps.enter_context(nc.psum_tensor(_u("a_pt"), [32, 128], BF16)) for _ in range(2)]

        @blk.tensor
        def _(t):
            idx = 0
            for c in range(32):
                for kind in range(2):
                    for p in range(8):
                        if idx >= 2:
                            t.wait_ge(S.h["atr_ev"], b["atr_ev"] + idx - 1)
                        src = P["qtok"] if kind == 0 else P["ktok"]
                        ins = t.transpose(pt[idx % 2][:, :],
                                          src[:, c, 32 * p:32 * p + 32], P["idb"][:, :])
                        S.inc("atr_mm", ins, 1)
                        idx += 1

        @blk.vector
        def _(v):
            idx = 0
            for c in range(32):
                for kind in range(2):
                    for p in range(8):
                        v.wait_ge(S.h["atr_mm"], b["atr_mm"] + idx + 1)
                        base = 32 * (p % 4)
                        tn = p // 4
                        if kind == 0:
                            for hi in range(2):
                                ins = v.tensor_tensor(
                                    P["qfT"][tn][base:base + 32, hi, c * 128:(c + 1) * 128],
                                    pt[idx % 2][:, :], P["mhalf"][0:32, hi, :], ALU.mult)
                        else:
                            ins = v.tensor_copy(
                                P["kfT"][tn][base:base + 32, c * 128:(c + 1) * 128],
                                pt[idx % 2][:, :])
                        S.inc("atr_ev", ins, 1)
                        idx += 1


def emit_attn0_core(nc, S, P, xin_dram, xout_dram, dbg=None):
    b = S.base("a_smm", "a_sev", "a_dg", "a_omm", "a_zr", "a_zk", "a_zbev",
               "a_oev", "a_wo", "epx", "epv", "eps")
    ents = []
    smm = sev = dg = omm = 0
    for i in range(8):
        for h in range(16):
            njc = 4 * i + 4
            tiles = []
            for jc in range(njc):
                diag = jc >= 4 * i
                tiles.append(dict(jc=jc, diag=diag, smm=smm, sev=sev, dg=dg, omm=omm + jc))
                smm += 1
                sev += 1
                if diag:
                    dg += 1
            ents.append(dict(i=i, h=h, njc=njc, tiles=tiles, ei=i * 16 + h, om0=omm))
            omm += njc

    with ExitStack() as ps:
      with nc.Block() as blk:
        sb = lambda n, sh, dt: ps.enter_context(nc.sbuf_tensor(_u(n), sh, dt))
        ssq = sb("a_ssq", [128, 3, 512], BF16)
        o_i = sb("a_oi", [128, 8, 512], BF16)
        zrec = sb("a_zrec", [1, 512], F32)
        zb = sb("a_zb", [64, 512], F32)
        xf = sb("a_xf", [128, 8, 512], F32)
        outst = sb("a_outst", [128, 8, 512], F32)
        ps_s = [ps.enter_context(nc.psum_tensor(_u("a_ps"), [128, 512], F32)) for _ in range(3)]
        ps_o = ps.enter_context(nc.psum_tensor(_u("a_po"), [128, 512], F32))
        pz = ps.enter_context(nc.psum_tensor(_u("a_pz"), [64, 512], F32))
        py = [ps.enter_context(nc.psum_tensor(_u("a_py"), [128, 512], F32)) for _ in range(2)]

        def o_wait_emit(t, ent, k):
            te = ent["tiles"][k]
            if te["diag"]:
                t.wait_ge(S.h["a_dg"], b["a_dg"] + te["dg"] + 1)
            else:
                t.wait_ge(S.h["a_sev"], b["a_sev"] + te["sev"] + 1)
            if k == 0:
                if ent["ei"] >= 1:
                    t.wait_ge(S.h["a_oev"], b["a_oev"] + ent["ei"])
            h = ent["h"]
            ins = t.matmul(ps_o[0:65, :],
                           P["v65"][:, te["jc"], h, 0:65],
                           ssq[:, te["sev"] % 3, :],
                           start=(k == 0), stop=(k == ent["njc"] - 1))
            S.inc("a_omm", ins, 1)

        @blk.tensor
        def _(t):
            for ent in ents:
                i, h = ent["i"], ent["h"]
                p = h // 2
                hi = h % 2
                base = 32 * (p % 4)
                tn = p // 4
                for jc in range(ent["njc"]):
                    te = ent["tiles"][jc]
                    if te["sev"] >= 3:
                        t.wait_ge(S.h["a_sev"], b["a_sev"] + te["sev"] - 2)
                    ins = t.matmul(ps_s[te["sev"] % 3][:, :],
                                   P["kfT"][tn][base:base + 32, jc * 128:(jc + 1) * 128],
                                   P["qfT"][tn][base:base + 32, hi, i * 512:(i + 1) * 512],
                                   start=True, stop=True, tile_position=(base, 0))
                    S.inc("a_smm", ins, 1)
                    if jc >= 2:
                        o_wait_emit(t, ent, jc - 2)
                o_wait_emit(t, ent, ent["njc"] - 2)
                o_wait_emit(t, ent, ent["njc"] - 1)
                t.wait_ge(S.h["a_zr"], b["a_zr"] + ent["ei"] + 1)
                if b["a_zbev"] + ent["ei"] >= 1:
                    t.wait_ge(S.h["a_zbev"], b["a_zbev"] + ent["ei"])
                ins = t.matmul(pz[:, :], P["ones1"][0:1, 0:64], zrec[0:1, :],
                               start=True, stop=True)
                S.inc("a_zk", ins, 1)
                if h == 15:
                    t.wait_ge(S.h["a_oev"], b["a_oev"] + 16 * (i + 1))
                    for m in range(8):
                        gm = 8 * i + m
                        if gm >= 2:
                            t.wait_ge(S.h["epv"], b["epv"] + gm - 1)
                        ins = None
                        for oc in range(8):
                            ins = t.matmul(py[m % 2][:, :],
                                           P["wo_sb"][:, oc, m * 128:(m + 1) * 128],
                                           o_i[:, oc, :],
                                           start=(oc == 0), stop=(oc == 7))
                        S.inc("a_wo", ins, 1)

        @blk.scalar
        def _(s):
            for ent in ents:
                for te in ent["tiles"]:
                    s.wait_ge(S.h["a_smm"], b["a_smm"] + te["smm"] + 1)
                    if te["sev"] >= 3:
                        s.wait_ge(S.h["a_omm"], b["a_omm"] + omm_of_sev(ents, te["sev"] - 3) + 1)
                    ins = s.activation(out=ssq[:, te["sev"] % 3, :],
                                       in_=ps_s[te["sev"] % 3][:, :], func=AF.Square)
                    S.inc("a_sev", ins, 1)

        @blk.vector
        def _(v):
            FL = lambda ins: S.flush(v, "vfl", ins)
            for ent in ents:
                i, h = ent["i"], ent["h"]
                if h == 0 and i >= 1:
                    v.wait_ge(S.h["a_wo"], b["a_wo"] + 8 * i)
                for te in ent["tiles"]:
                    if te["diag"]:
                        v.wait_ge(S.h["a_sev"], b["a_sev"] + te["sev"] + 1)
                        r = te["jc"] - 4 * i
                        ins = v.tensor_tensor(ssq[:, te["sev"] % 3, :],
                                              ssq[:, te["sev"] % 3, :],
                                              P["maskb"][:, r, :], ALU.mult)
                        S.inc("a_dg", ins, 1)
                v.wait_ge(S.h["a_omm"], b["a_omm"] + ent["om0"] + ent["njc"])
                v.tensor_single_scalar(zrec[0:1, :], ps_o[64:65, :], LN_EPS, ALU.add)
                ins = v.reciprocal(zrec[0:1, :], zrec[0:1, :])
                S.inc("a_zr", ins, 1)
                v.wait_ge(S.h["a_zk"], b["a_zk"] + ent["ei"] + 1)
                ins = v.tensor_copy(zb[:, :], pz[0:64, :])
                S.inc("a_zbev", ins, 1)
                tn2 = h // 2
                ins = v.tensor_tensor(o_i[64 * (h % 2):64 * (h % 2) + 64, tn2, :],
                                      ps_o[0:64, :], zb[:, :], ALU.mult)
                S.inc("a_oev", ins, 1)
                if h == 15:
                    if i >= 1:
                        v.wait_ge(S.h["eps"], b["eps"] + 16 * i)
                    for m in range(8):
                        gm = 8 * i + m
                        v.wait_ge(S.h["a_wo"], b["a_wo"] + gm + 1)
                        v.wait_ge(S.h["epx"], b["epx"] + 16 * (i + 1))
                        ins = v.scalar_tensor_tensor(outst[:, m, :], py[m % 2][:, :], 1.0,
                                                     xf[:, m, :], ALU.mult, ALU.add)
                        S.inc("epv", ins, 1)

        @blk.sync
        def _(sy):
            for i in range(8):
                if i >= 1:
                    sy.wait_ge(S.h["epv"], b["epv"] + 8 * i)
                S.inc("epx", sy.dma_start(out=xf[:, :, :],
                                          in_=xslice_ap(xin_dram, i * 512, 512)), 16)
                sy.wait_ge(S.h["epv"], b["epv"] + 8 * (i + 1))
                S.inc("eps", sy.dma_start(
                    out=xslice_ap(xout_dram, i * 512, 512), in_=outst[:, :, :]), 16)

      if dbg is not None:
        _dbg_dump(nc, S, o_i, dbg["oi"])


def omm_of_sev(ents, sev_idx):
    for ent in ents:
        for te in ent["tiles"]:
            if te["sev"] == sev_idx:
                return te["omm"]
    raise KeyError(sev_idx)


def build_attn0_nc(debug=False):
    nc = bass.Bass("TRN2", target_bir_lowering=False, detect_race_conditions=False)
    xin = nc.dram_tensor("xin", [D, T], F32, kind="ExternalInput")
    ident = nc.dram_tensor("ident", [128, 128], F32, kind="ExternalInput")
    WqT = nc.dram_tensor("WqT", [D, 256], F32, kind="ExternalInput")
    WkT = nc.dram_tensor("WkT", [D, 256], F32, kind="ExternalInput")
    WvT = nc.dram_tensor("WvT", [D, D], F32, kind="ExternalInput")
    WoT = nc.dram_tensor("WoT", [D, D], F32, kind="ExternalInput")
    gb = nc.dram_tensor("gb", [2, 128, 256], F32, kind="ExternalInput")
    masks = nc.dram_tensor("masks", [128, 4, 512], F32, kind="ExternalInput")
    mhalf = nc.dram_tensor("mhalf", [32, 2, 128], F32, kind="ExternalInput")
    xout = nc.dram_tensor("xout", [D, T], F32, kind="ExternalOutput")
    dbg = {}
    if debug:
        for nm, sh, dt in [("gq", [128, 256], F32), ("gk", [128, 256], F32),
                           ("bq", [128, 256], F32), ("bk", [128, 256], F32),
                           ("m16", [128, 16], F32), ("v16", [128, 16], F32),
                           ("rsd", [128, 16], F32), ("srt", [128, 16], F32),
                           ("qd", [128, 2, 256], F32), ("t1", [128, 256], F32),
                           ("qtok", [128, 32, 256], BF16), ("ktok", [128, 32, 256], BF16),
                           ("v65", [128, 32, 16, 65], BF16),
                           ("qfT0", [128, 2, 4096], BF16), ("kfT0", [128, 4096], BF16),
                           ("oi", [128, 8, 512], BF16)]:
            dbg[nm] = nc.dram_tensor("dbg_" + nm, sh, dt, kind="ExternalOutput")

    with ExitStack() as st:
        S = Sems(nc, st, FFN_SEMS + ATTN0_SEMS)
        sb = lambda n, sh, dt: st.enter_context(nc.sbuf_tensor(_u(n), sh, dt))
        sbr = lambda n, sh, dt: st.enter_context(nc.sbuf_tensor(_u(n), sh, dt, side="right"))
        P = dict(
            ones1=sb("ones1", [128, 128], F32),
            idb=sb("idb", [128, 128], BF16),
            gq_bc=sb("gq_bc", [128, 256], F32),
            gk_bc=sb("gk_bc", [128, 256], F32),
            bq_bc=sb("bq_bc", [128, 256], F32),
            bk_bc=sb("bk_bc", [128, 256], F32),
            v65=sb("v65", [128, 32, 16, 65], BF16),
            maskb=sb("maskb", [128, 4, 512], BF16),
            mhalf=sb("mhalf", [32, 2, 128], BF16),
        )
        stQ = ExitStack()
        P["qtok"] = stQ.enter_context(nc.sbuf_tensor(_u("qtok"), [128, 32, 256], BF16, side="right"))
        P["ktok"] = stQ.enter_context(nc.sbuf_tensor(_u("ktok"), [128, 32, 256], BF16, side="right"))
        with nc.Block() as blk:
            @blk.gpsimd
            def _(g):
                g.memset(P["ones1"][:, :], 1.0)
                g.memset(P["v65"][:, :, :, :], 1.0)
                S.inc("dg", g.dma_start(out=P["idb"][:, :], in_=ident.ap()), 16)
                S.inc("dg", g.dma_start(out=P["gk_bc"][:, :], in_=bass.AP(gb, 0, [[256, 128], [1, 256]])), 16)
                S.inc("dg", g.dma_start(out=P["bk_bc"][:, :], in_=bass.AP(gb, 128 * 256, [[256, 128], [1, 256]])), 16)
                S.inc("dg", g.dma_start(out=P["maskb"][:, :, :], in_=masks.ap()), 16)
                S.inc("dg", g.dma_start(out=P["mhalf"][:, :, :], in_=mhalf.ap()), 16)
                g.wait_ge(S.h["dg"], S.c["dg"])

            @blk.vector
            def _(v):
                v.wait_ge(S.h["dg"], S.c["dg"])
                v.tensor_single_scalar(P["gq_bc"][:, :], P["gk_bc"][:, :], 0.25, ALU.mult)
                v.tensor_single_scalar(P["bq_bc"][:, :], P["bk_bc"][:, :], 0.25, ALU.mult)

        with ExitStack() as stB:
            sb2 = lambda n, sh, dt: stB.enter_context(nc.sbuf_tensor(_u(n), sh, dt))
            xn = sb2("a_xn", [128, 8, 2048], BF16)
            P["wq_sb"] = sb2("wq_sb", [128, 8, 256], BF16)
            P["wk_sb"] = sb2("wk_sb", [128, 8, 256], BF16)
            P["wv_sb"] = sb2("wv_sb", [128, 8, 1024], BF16)
            with nc.Block() as blk:
                @blk.gpsimd
                def _(g):
                    S.inc("dg", g.dma_start(out=P["wq_sb"][:, :, :],
                                            in_=ap3(WqT, 0, (256, 128), (128 * 256, 8), (1, 256))), 16)
                    S.inc("dg", g.dma_start(out=P["wk_sb"][:, :, :],
                                            in_=ap3(WkT, 0, (256, 128), (128 * 256, 8), (1, 256))), 16)
                    S.inc("dg", g.dma_start(out=P["wv_sb"][:, :, :],
                                            in_=ap3(WvT, 0, (D, 128), (128 * D, 8), (1, D))), 16)
                    g.wait_ge(S.h["dg"], S.c["dg"])
            for m in range(2):
                emit_ln(nc, S, {"ones1": P["ones1"]}, xin, xn, m, 4)
                emit_attn0_qkv(nc, S, P, xn, c0=16 * m, nc_chunks=16,
                               dbg=(dbg if (debug and m == 1) else None))
            if debug:
                _dbg_dump(nc, S, P["gq_bc"], dbg["gq"])
                _dbg_dump(nc, S, P["gk_bc"], dbg["gk"])
                _dbg_dump(nc, S, P["bq_bc"], dbg["bq"])
                _dbg_dump(nc, S, P["bk_bc"], dbg["bk"])
                _dbg_dump(nc, S, P["qtok"], dbg["qtok"])
                _dbg_dump(nc, S, P["ktok"], dbg["ktok"])
                _dbg_dump(nc, S, P["v65"], dbg["v65"])

        P["qfT"] = [sb("qfT0", [128, 2, 4096], BF16), sb("qfT1", [128, 2, 4096], BF16)]
        P["kfT"] = [sb("kfT0", [128, 4096], BF16), sb("kfT1", [128, 4096], BF16)]
        emit_attn0_tr(nc, S, P)
        if debug:
            _dbg_dump(nc, S, P["qfT"][0], dbg["qfT0"])
            _dbg_dump(nc, S, P["kfT"][0], dbg["kfT0"])
        stQ.close()
        P["wo_sb"] = sb("wo_sb", [128, 8, 1024], BF16)
        with nc.Block() as blk:
            @blk.gpsimd
            def _(g):
                S.inc("dg", g.dma_start(out=P["wo_sb"][:, :, :],
                                        in_=ap3(WoT, 0, (D, 128), (128 * D, 8), (1, D))), 16)
                g.wait_ge(S.h["dg"], S.c["dg"])
        emit_attn0_core(nc, S, P, xin, xout, dbg=(dbg if debug else None))
        with nc.Block() as blk:
            @blk.sync
            def _(sy):
                sy.wait_ge(S.h["eps"], S.c["eps"])
    return nc


ATTN1_SEMS = ["sfl", "b_mm", "b_ev", "b_tr", "b_kst", "b_trc", "b_vst", "b_kv",
              "b_kdma", "b_sq", "b_mh", "b_pm", "b_smm", "b_sev", "b_dg",
              "b_omm", "b_oacc", "b_zr", "b_zk", "b_zbev", "b_wo", "b_qtr", "b_qn", "b_qns",
              "b_qtrc", "b_kld", "b_kvdone", "b_pre", "b_prv", "b_kvd2", "b_mtr", "b_mtrc", "b_mtr2"]


def emit_attn1_qkv(nc, S, P, h, kT_sc, v65_sc, c0, nchunks):
    b = S.base("b_mm", "b_ev", "b_tr", "b_kst", "b_trc", "b_vst", "b_kv",
               "b_kdma", "b_sq", "b_qn", "b_qns")
    with ExitStack() as ps:
      with nc.Block() as blk:
        sb = lambda n, sh, dt: ps.enter_context(nc.sbuf_tensor(_u(n), sh, dt))
        sc1 = sb("b_sc1", [128, 16, 32], F32)
        sc2 = sb("b_sc2", [128, 16, 32], F32)

        ktmp = sb("b_ktmp", [128, 1024], BF16)
        kstage = sb("b_kstage", [128, 8, 128], BF16)
        vstage = sb("b_vstage", [128, 16, 65], BF16)
        pk = ps.enter_context(nc.psum_tensor(_u("b_pqkv"), [128, 6, 512], F32))
        ptr = [ps.enter_context(nc.psum_tensor(_u("b_ptr"), [128, 128], BF16))
               for _ in range(2)]

        @blk.gpsimd
        def _(g):
            g.memset(vstage[:, :, :], 1.0)

        @blk.tensor
        def _(t):
            for cc in range(nchunks):
                if cc >= 1:
                    t.wait_ge(S.h["b_vst"], b["b_vst"] + cc)
                ins = None
                for K in range(8):
                    for n6 in range(6):
                        ins = t.matmul(pk[:, n6, :], h[:, K, cc * 128:(cc + 1) * 128],
                                       P["wqkv_sb"][:, K, n6 * 512:(n6 + 1) * 512],
                                       start=(K == 0), stop=(K == 7))
                S.inc("b_mm", ins, 1)
                t.wait_ge(S.h["b_tr"], b["b_tr"] + cc + 1)
                for g2 in range(8):
                    if cc * 8 + g2 >= 2:
                        t.wait_ge(S.h["b_trc"], b["b_trc"] + cc * 8 + g2 - 1)
                    ins = t.transpose(ptr[g2 % 2][:, :], ktmp[:, g2 * 128:(g2 + 1) * 128],
                                      P["idb"][:, :])
                    S.inc("b_kst", ins, 1)

        @blk.vector
        def _(v):
            FL = lambda ins: S.flush(v, "vfl", ins)
            for cc in range(nchunks):
                c = c0 + cc
                v.wait_ge(S.h["b_mm"], b["b_mm"] + cc + 1)
                cosb = P["cos_sb"][:, c, :].rearrange("p (h u) -> p h u", h=1)\
                    .broadcast_to((128, 16, 32))
                sinb = P["sin_sb"][:, c, :].rearrange("p (h u) -> p h u", h=1)\
                    .broadcast_to((128, 16, 32))
                for sec in range(2):
                    A = pk[:, 2 * sec:2 * sec + 2, :].rearrange(
                        "p a (h2 u two) -> p (a h2) u two", h2=8, u=32, two=2)
                    E = A[:, :, :, 0]
                    O = A[:, :, :, 1]
                    dst = (P["q_tok"][:, c, :] if sec == 0 else ktmp[:, :])
                    DD = dst.rearrange("p (h u two) -> p h u two", h=16, u=32, two=2)
                    v.tensor_tensor(sc1[:, :, :], E, cosb, ALU.mult)
                    v.tensor_tensor(sc2[:, :, :], O, sinb, ALU.mult)
                    v.tensor_tensor(DD[:, :, :, 0], sc1[:, :, :], sc2[:, :, :], ALU.subtract)
                    v.tensor_tensor(sc1[:, :, :], O, cosb, ALU.mult)
                    v.tensor_tensor(sc2[:, :, :], E, sinb, ALU.mult)
                    ins = v.tensor_tensor(DD[:, :, :, 1], sc1[:, :, :], sc2[:, :, :], ALU.add)
                    if sec == 1:
                        S.inc("b_tr", ins, 1)
                if cc >= 1:
                    v.wait_ge(S.h["b_kdma"], b["b_kdma"] + 32 * cc)
                ins = v.tensor_copy(vstage[:, :, 0:64],
                                    pk[:, 4:6, :].rearrange("p a (h2 f) -> p (a h2) f", h2=8))
                S.inc("b_vst", ins, 1)
                for g2 in range(8):
                    v.wait_ge(S.h["b_kst"], b["b_kst"] + cc * 8 + g2 + 1)
                    ins = v.tensor_copy(kstage[:, g2, :], ptr[g2 % 2][:, :])
                    S.inc("b_trc", ins, 1)
                S.inc("b_kv", v.nop(), 1)

        @blk.sync
        def _(sy):
            for cc in range(nchunks):
                c = c0 + cc
                sy.wait_ge(S.h["b_kv"], b["b_kv"] + cc + 1)
                S.inc("b_kdma", sy.dma_start(
                    out=ap3(kT_sc, c * 128, (T, 128), (128 * T, 8), (1, 128)),
                    in_=kstage[:, :, :]), 16)
                sy.wait_ge(S.h["b_vst"], b["b_vst"] + cc + 1)
                S.inc("b_kdma", sy.dma_start(
                    out=bass.AP(v65_sc, c * 128 * 1040, [[1040, 128], [1, 1040]]),
                    in_=vstage[:, :, :].rearrange("p h f -> p (h f)")), 16)


def emit_attn1_core(nc, S, P, xin_dram, xout_dram, kT_sc, v65_sc, dbg=None):
    for i in range(8):
        _attn1_core_i(nc, S, P, xin_dram, xout_dram, kT_sc, v65_sc, i, dbg=dbg)


def _attn1_core_i(nc, S, P, xin_dram, xout_dram, kT_sc, v65_sc, i, dbg=None):
    bA = S.base("b_qtr", "b_qtrc")
    with ExitStack() as ps:
      with nc.Block() as blk:
        pqt = [ps.enter_context(nc.psum_tensor(_u("b_pqt"), [128, 128], BF16))
               for _ in range(2)]

        @blk.tensor
        def _(t):
            idx = 0
            for ci in range(4):
                for bb in range(8):
                    if idx >= 2:
                        t.wait_ge(S.h["b_qtrc"], bA["b_qtrc"] + idx - 1)
                    ins = t.transpose(pqt[idx % 2][:, :],
                                      P["q_tok"][:, 4 * i + ci, 128 * bb:128 * (bb + 1)],
                                      P["idb"][:, :])
                    S.inc("b_qtr", ins, 1)
                    idx += 1

        @blk.vector
        def _(v):
            idx = 0
            for ci in range(4):
                for bb in range(8):
                    v.wait_ge(S.h["b_qtr"], bA["b_qtr"] + idx + 1)
                    ins = v.tensor_copy(P["qT"][:, bb, ci * 128:(ci + 1) * 128],
                                        pqt[idx % 2][:, :])
                    S.inc("b_qtrc", ins, 1)
                    idx += 1

    bP = S.base("b_pre", "b_prv", "b_kld", "b_kvd2", "b_mtr", "b_mtrc", "b_mtr2", "b_kdma")
    with ExitStack() as ps:
      with nc.Block() as blk:
        sb = lambda n, sh, dt: ps.enter_context(nc.sbuf_tensor(_u(n), sh, dt))
        ng = i + 2
        rmxall = sb("b_rmxall", [128, 4, 16, ng], F32)
        rmxs = sb("b_rmxs", [128, 4, 16], F32)
        pps = [ps.enter_context(nc.psum_tensor(_u("b_pps"), [128, 512], F32))
               for _ in range(3)]
        pmt = [ps.enter_context(nc.psum_tensor(_u("b_pmt"), [16, 128], F32))
               for _ in range(2)]

        @blk.sync
        def _(sy):
            if i == 0:
                sy.wait_ge(S.h["b_kdma"], bP["b_kdma"])
            for g in range(i + 1):
                if g >= 2:
                    sy.wait_ge(S.h["b_kvd2"], bP["b_kvd2"] + g - 1)
                S.inc("b_kld", sy.dma_start(
                    out=P["kbuf"][g % 2][:, :, :],
                    in_=ap3(kT_sc, g * 512, (T, 128), (128 * T, 8), (1, 512))), 16)

        @blk.tensor
        def _(t):
            idx = 0
            for gi2 in range(i + 2):
                if gi2 >= 1:
                    t.wait_ge(S.h["b_kld"], bP["b_kld"] + 16 * gi2)
                ins = None
                for ci in range(4):
                    for h in range(16):
                        base = 64 * (h % 2)
                        if idx >= 3:
                            t.wait_ge(S.h["b_prv"], bP["b_prv"] + idx - 2)
                        lh = P["qT"][base:base + 64, h // 2, ci * 128:(ci + 1) * 128]
                        if gi2 == 0:
                            ins = t.matmul(pps[idx % 3][:, 0:4], lh,
                                           P["pmT_sb"][base:base + 64, h // 2, :],
                                           start=True, stop=True,
                                           tile_position=(base, 0))
                        else:
                            ins = t.matmul(pps[idx % 3][:, :], lh,
                                           P["kbuf"][(gi2 - 1) % 2][base:base + 64,
                                                                    h // 2, :],
                                           start=True, stop=True,
                                           tile_position=(base, 0))
                        S.inc("b_pre", ins, 1)
                        idx += 1
                if gi2 >= 1:
                    S.inc("b_kvd2", t.nop(), 1)
            for ci in range(4):
                t.wait_ge(S.h["b_mtr"], bP["b_mtr"] + ci + 1)
                if bP["b_mtrc"] + ci >= 2:
                    t.wait_ge(S.h["b_mtrc"], bP["b_mtrc"] + ci - 1)
                ins = t.transpose(pmt[ci % 2][:, :], rmxs[:, ci, :], P["idf"][:, :])
                S.inc("b_mtr2", ins, 1)

        @blk.vector
        def _(v):
            FL = lambda ins: S.flush(v, "vfl", ins)
            idx = 0
            for gi2 in range(i + 2):
                for ci in range(4):
                    for h in range(16):
                        v.wait_ge(S.h["b_pre"], bP["b_pre"] + idx + 1)
                        src = pps[idx % 3][:, 0:4] if gi2 == 0 else pps[idx % 3][:, :]
                        ins = v.reduce_max(rmxall[:, ci, h, gi2:gi2 + 1], src, AX.X)
                        S.inc("b_prv", ins, 1)
                        idx += 1
            for ci in range(4):
                FL(v.tensor_reduce(rmxs[:, ci, :], rmxall[:, ci, :, :], AX.X, ALU.max))
                ins = v.tensor_single_scalar(rmxs[:, ci, :], rmxs[:, ci, :], -1.0,
                                             ALU.mult)
                S.xinc(v, "vfl", ins, "b_mtr", 1)
                v.wait_ge(S.h["b_mtr2"], bP.get("b_mtr2", S.c["b_mtr2"] - 4) + ci + 1) if False else None
            for ci in range(4):
                v.wait_ge(S.h["b_mtr2"], bP["b_mtr2"] + ci + 1)
                ins = v.tensor_copy(P["mT"][0:16, ci * 128:(ci + 1) * 128],
                                    pmt[ci % 2][0:16, :])
                S.inc("b_mtrc", ins, 1)

    glist = ["pm"] + list(range(i + 1))
    b = S.base("b_smm", "b_sev", "b_dg", "b_omm", "b_oacc", "b_zr", "b_zk",
               "b_zbev", "b_kld", "b_kvdone", "b_kdma")
    with ExitStack() as ps:
      with nc.Block() as blk:
        sb = lambda n, sh, dt: ps.enter_context(nc.sbuf_tensor(_u(n), sh, dt))
        ee = sb("b_e", [128, 3, 512], BF16)
        epm = sb("b_epm", [4, 512], BF16)
        P["_ee"] = ee
        P["_epm"] = epm
        zrec = sb("b_zrec", [1, 512], F32)
        zb = sb("b_zb", [64, 512], F32)
        ps_s = [ps.enter_context(nc.psum_tensor(_u("b_ps"), [128, 512], F32))
                for _ in range(3)]
        ps_pm = ps.enter_context(nc.psum_tensor(_u("b_ppm"), [4, 512], F32))
        po = [ps.enter_context(nc.psum_tensor(_u("b_po"), [128, 512], F32))
              for _ in range(2)]
        pz = ps.enter_context(nc.psum_tensor(_u("b_pz"), [64, 512], F32))

        nsm = [0]; nse = [0]; ndg = [0]; nom = [0]; nacc = [0]
        sched = []
        for gi, g in enumerate(glist):
            for h in range(16):
                ent = dict(g=g, gi=gi, h=h, sm0=nsm[0], se0=nse[0], dg0=ndg[0],
                           om0=nom[0], ac0=nacc[0])
                if g == "pm":
                    nsm[0] += 1; nse[0] += 1; nom[0] += 1
                else:
                    nsm[0] += 4; nse[0] += 4; nom[0] += 4
                    if g == i:
                        ndg[0] += 4
                nacc[0] += 1
                sched.append(ent)

        @blk.sync
        def _(sy):
            if i == 0:
                sy.wait_ge(S.h["b_kdma"], b["b_kdma"])
            for gi, g in enumerate(glist):
                if g == "pm":
                    continue
                gg = gi - 1
                if gg >= 2:
                    sy.wait_ge(S.h["b_kvdone"], b["b_kvdone"] + gg - 1)
                S.inc("b_kld", sy.dma_start(
                    out=P["kbuf"][gg % 2][:, :, :],
                    in_=ap3(kT_sc, g * 512, (T, 128), (128 * T, 8), (1, 512))), 16)
                S.inc("b_kld", sy.dma_start(
                    out=P["vbuf"][gg % 2][:, :, :, :].rearrange("p q h f -> p q (h f)"),
                    in_=ap3(v65_sc, g * 512 * 1040, (1040, 128), (128 * 1040, 4), (1, 1040))
                    ), 16)

        @blk.tensor
        def _(t):
            for ent in sched:
                g, gi, h = ent["g"], ent["gi"], ent["h"]
                if g == "pm":
                    if ent["sm0"] >= 1:
                        t.wait_ge(S.h["b_sev"], b["b_sev"] + ent["se0"])
                    base = 64 * (h % 2)
                    ins = t.matmul(ps_pm[:, :], P["pmT_sb"][base:base + 64, h // 2, :],
                                   P["qT"][base:base + 64, h // 2, :],
                                   start=True, stop=False, tile_position=(base, 0))
                    ins = t.matmul(ps_pm[:, :], P["sel16_sb"][0:16, h, 0:4],
                                   P["mT"][0:16, :], start=False, stop=True)
                    S.inc("b_smm", ins, 1)
                    t.wait_ge(S.h["b_sev"], b["b_sev"] + ent["se0"] + 1)
                    if ent["ac0"] >= 1:
                        t.wait_ge(S.h["b_oacc"], b["b_oacc"] + ent["ac0"] - 1)
                    ins = t.matmul(po[ent["ac0"] % 2][0:65, :],
                                   P["pmv65_sb"][0:4, h, 0:65],
                                   epm[0:4, :], start=True, stop=True)
                    S.inc("b_omm", ins, 1)
                else:
                    gg = gi - 1
                    if h == 0:
                        t.wait_ge(S.h["b_kld"], b["b_kld"] + 32 * (gg + 1))
                    base = 64 * (h % 2)
                    for jc in range(4):
                        sm = ent["sm0"] + jc
                        if sm >= 3:
                            t.wait_ge(S.h["b_sev"], b["b_sev"] + sm - 2)
                        t.matmul(ps_s[sm % 3][:, :],
                                 P["kbuf"][gg % 2][base:base + 64, h // 2,
                                                   jc * 128:(jc + 1) * 128],
                                 P["qT"][base:base + 64, h // 2, :],
                                 start=True, stop=False,
                                 tile_position=(base, 0))
                        ins = t.matmul(ps_s[sm % 3][:, :], P["sel16_sb"][0:16, h, :],
                                       P["mT"][0:16, :], start=False, stop=True)
                        S.inc("b_smm", ins, 1)
                    if ent["ac0"] >= 1:
                        t.wait_ge(S.h["b_oacc"], b["b_oacc"] + ent["ac0"] - 1)
                    for jc in range(4):
                        se = ent["se0"] + jc
                        if g == i:
                            t.wait_ge(S.h["b_dg"], b["b_dg"] + ent["dg0"] + jc + 1)
                        else:
                            t.wait_ge(S.h["b_sev"], b["b_sev"] + se + 1)
                        ins = t.matmul(po[ent["ac0"] % 2][0:65, :],
                                       P["vbuf"][gg % 2][:, jc, h, 0:65],
                                       ee[:, se % 3, :],
                                       start=(jc == 0), stop=(jc == 3))
                        S.inc("b_omm", ins, 1)
                    if h == 15:
                        S.inc("b_kvdone", t.nop(), 1)
                if ent["h"] == 15 and gi == len(glist) - 1:
                    for hh in range(16):
                        t.wait_ge(S.h["b_zr"], b["b_zr"] + hh + 1)
                        if b["b_zbev"] + hh >= 1:
                            t.wait_ge(S.h["b_zbev"], b["b_zbev"] + hh)
                        ins = t.matmul(pz[:, :], P["ones1"][0:1, 0:64], zrec[0:1, :],
                                       start=True, stop=True)
                        S.inc("b_zk", ins, 1)

        @blk.scalar
        def _(s):
            for ent in sched:
                g, h = ent["g"], ent["h"]
                if g == "pm":
                    s.wait_ge(S.h["b_smm"], b["b_smm"] + ent["sm0"] + 1)
                    ins = s.activation(out=epm[0:4, :], in_=ps_pm[0:4, :], func=AF.Exp,
                                       scale=0.125)
                    S.xinc(s, "sfl", ins, "b_sev", 1)
                else:
                    for jc in range(4):
                        sm = ent["sm0"] + jc
                        se = ent["se0"] + jc
                        s.wait_ge(S.h["b_smm"], b["b_smm"] + sm + 1)
                        if se >= 3:
                            s.wait_ge(S.h["b_omm"], b["b_omm"] + omm_of_se(sched, se - 3) + 1)
                        ins = s.activation(out=ee[:, se % 3, :], in_=ps_s[sm % 3][:, :],
                                           func=AF.Exp, scale=0.125)
                        S.xinc(s, "sfl", ins, "b_sev", 1)

        @blk.vector
        def _(v):
            FL = lambda ins: S.flush(v, "vfl", ins)
            for ent in sched:
                g, gi, h = ent["g"], ent["gi"], ent["h"]
                if g == i and g != "pm":
                    for jc in range(4):
                        se = ent["se0"] + jc
                        v.wait_ge(S.h["b_sev"], b["b_sev"] + se + 1)
                        ins = v.tensor_tensor(ee[:, se % 3, :], ee[:, se % 3, :],
                                              P["maskb"][:, jc, :], ALU.mult)
                        S.inc("b_dg", ins, 1)
                v.wait_ge(S.h["b_omm"], b["b_omm"] + ent["om0"] +
                          (1 if g == "pm" else 4))
                dst = P["oacc"][0:65, h, :]
                if g == "pm":
                    ins = v.tensor_copy(dst, po[ent["ac0"] % 2][0:65, :])
                else:
                    ins = v.scalar_tensor_tensor(dst, po[ent["ac0"] % 2][0:65, :], 1.0,
                                                 dst, ALU.mult, ALU.add)
                S.inc("b_oacc", ins, 1)
            for hh in range(16):
                v.tensor_single_scalar(zrec[0:1, :], P["oacc"][64:65, hh, :],
                                       1e-30, ALU.max)
                ins = v.reciprocal(zrec[0:1, :], zrec[0:1, :])
                S.xinc(v, "vfl", ins, "b_zr", 1)
                v.wait_ge(S.h["b_zk"], b["b_zk"] + hh + 1)
                ins = v.tensor_copy(zb[:, :], pz[0:64, :])
                S.inc("b_zbev", ins, 1)
                v.tensor_tensor(P["o_i"][64 * (hh % 2):64 * (hh % 2) + 64, hh // 2, :],
                                P["oacc"][0:64, hh, :], zb[:, :], ALU.mult)

    if dbg is not None and i == 0:
        with nc.Block() as blkD:
            @blkD.sync
            def _(sy):
                S.inc("eps", sy.dma_start(out=dbg["oacc"].ap()[0:65, :, :],
                                          in_=P["oacc"][0:65, :, :]), 16)
                sy.wait_ge(S.h["eps"], S.c["eps"])
        _dbg_dump(nc, S, P["qT"], dbg["qT"])
        _dbg_dump(nc, S, P["_ee"], dbg["ee"])
        _dbg_dump(nc, S, P["_epm"], dbg["epm"])
        _dbg_dump(nc, S, P["kbuf"][0], dbg["kbuf"])
        _dbg_dump(nc, S, P["vbuf"][0], dbg["vbuf"])
        _dbg_dump(nc, S, P["o_i"], dbg["o_i"])

    bC = S.base("b_wo", "epx", "epv", "eps")
    with ExitStack() as ps:
      with nc.Block() as blk:
        sb = lambda n, sh, dt: ps.enter_context(nc.sbuf_tensor(_u(n), sh, dt))
        xf = sb("b_xf", [128, 4, 512], F32)
        outst = sb("b_outst", [128, 4, 512], F32)
        py = [ps.enter_context(nc.psum_tensor(_u("b_py"), [128, 512], F32))
              for _ in range(2)]

        @blk.tensor
        def _(t):
            for m in range(8):
                if m >= 2:
                    t.wait_ge(S.h["epv"], bC["epv"] + m - 1)
                ins = None
                for oc in range(8):
                    ins = t.matmul(py[m % 2][:, :],
                                   P["wout_sb"][:, oc, m * 128:(m + 1) * 128],
                                   P["o_i"][:, oc, :], start=(oc == 0), stop=(oc == 7))
                S.inc("b_wo", ins, 1)

        @blk.vector
        def _(v):
            for half in range(2):
                if i + half >= 1:
                    v.wait_ge(S.h["eps"], bC["eps"] - 16 + 16 * half)
                for mm in range(4):
                    m = 4 * half + mm
                    v.wait_ge(S.h["b_wo"], bC["b_wo"] + m + 1)
                    v.wait_ge(S.h["epx"], bC["epx"] + 16 * (half + 1))
                    ins = v.scalar_tensor_tensor(outst[:, mm, :], py[m % 2][:, :], 1.0,
                                                 xf[:, mm, :], ALU.mult, ALU.add)
                    S.inc("epv", ins, 1)

        @blk.sync
        def _(sy):
            for half in range(2):
                if half >= 1:
                    sy.wait_ge(S.h["epv"], bC["epv"] + 4)
                S.inc("epx", sy.dma_start(
                    out=xf[:, :, :],
                    in_=ap3(xin_dram, i * 512 + half * 4 * 128 * T,
                            (T, 128), (128 * T, 4), (1, 512))), 16)
                sy.wait_ge(S.h["epv"], bC["epv"] + 4 * (half + 1))
                S.inc("eps", sy.dma_start(
                    out=ap3(xout_dram, i * 512 + half * 4 * 128 * T,
                            (T, 128), (128 * T, 4), (1, 512)),
                    in_=outst[:, :, :]), 16)


def omm_of_se(sched, se_idx):
    for ent in sched:
        n = 1 if ent["g"] == "pm" else 4
        if ent["se0"] <= se_idx < ent["se0"] + n:
            return ent["om0"] + (se_idx - ent["se0"])
    raise KeyError(se_idx)


def build_attn1_nc(debug=False):
    nc = bass.Bass("TRN2", target_bir_lowering=False, detect_race_conditions=False)
    xin = nc.dram_tensor("xin", [D, T], F32, kind="ExternalInput")
    ident = nc.dram_tensor("ident", [128, 128], F32, kind="ExternalInput")
    wqkvT = nc.dram_tensor("wqkvT", [D, 3072], F32, kind="ExternalInput")
    woutT = nc.dram_tensor("woutT", [D, D], F32, kind="ExternalInput")
    nw = nc.dram_tensor("nw", [D], F32, kind="ExternalInput")
    pm = nc.dram_tensor("pm", [16, 4, 64], F32, kind="ExternalInput")
    pmT = nc.dram_tensor("pmT", [16, 64, 4], F32, kind="ExternalInput")
    cosr = nc.dram_tensor("cosr", [128, 32, 32], F32, kind="ExternalInput")
    sel16 = nc.dram_tensor("sel16", [16, 16, 128], F32, kind="ExternalInput")
    sinr = nc.dram_tensor("sinr", [128, 32, 32], F32, kind="ExternalInput")
    masks = nc.dram_tensor("masks", [128, 4, 512], F32, kind="ExternalInput")
    xout = nc.dram_tensor("xout", [D, T], F32, kind="ExternalOutput")
    kT_sc = nc.dram_tensor("kT_sc", [D, T], BF16, kind="Internal")
    v65_sc = nc.dram_tensor("v65_sc", [T, 1040], BF16, kind="Internal")
    dbg = {}
    if debug:
        for nm, sh, dt in [("oacc", [65, 16, 512], F32), ("qT", [128, 8, 512], BF16),
                           ("ee", [128, 3, 512], BF16), ("epm", [4, 512], BF16),
                           ("kbuf", [128, 8, 512], BF16), ("vbuf", [128, 4, 16, 65], BF16),
                           ("o_i", [128, 8, 512], BF16)]:
            dbg[nm] = nc.dram_tensor("dbg_" + nm, sh, dt, kind="ExternalOutput")

    with ExitStack() as st:
        S = Sems(nc, st, FFN_SEMS + ATTN1_SEMS)
        sb = lambda n, sh, dt: st.enter_context(nc.sbuf_tensor(_u(n), sh, dt))
        P = dict(
            ones1=sb("ones1", [128, 128], F32),
            idf=sb("idf", [128, 128], F32),
            idb=sb("idb", [128, 128], BF16),
            maskb=sb("maskb", [128, 4, 512], BF16),


            pmv_sb=sb("pmv_sb", [16, 4, 64], F32),
            pmT_sb=sb("pmT_sb", [128, 8, 4], BF16),
            pmv65_sb=sb("pmv65", [4, 16, 65], BF16),
            q_tok=sb("q_tok", [128, 32, 1024], BF16),
            nwcols=[sb(f"nwcol{dc}", [128, 1], F32) for dc in range(8)],
        )
        with nc.Block() as blk:
            @blk.gpsimd
            def _(g):
                g.memset(P["ones1"][:, :], 1.0)

                g.memset(P["pmv65_sb"][:, :, :], 1.0)
                S.inc("dg", g.dma_start(out=P["idf"][:, :], in_=ident.ap()), 16)
                S.inc("dg", g.dma_start(out=P["idb"][:, :], in_=ident.ap()), 16)
                S.inc("dg", g.dma_start(out=P["maskb"][:, :, :], in_=masks.ap()), 16)
                S.inc("dg", g.dma_start(out=P["pmv_sb"][:, :, :], in_=pm.ap()), 16)
                S.inc("dg", g.dma_start(
                    out=P["pmT_sb"][0:64, :, :],
                    in_=ap3(pmT, 0, (4, 64), (512, 8), (1, 4))), 16)
                S.inc("dg", g.dma_start(
                    out=P["pmT_sb"][64:128, :, :],
                    in_=ap3(pmT, 256, (4, 64), (512, 8), (1, 4))), 16)
                S.inc("dg", g.dma_start(
                    out=P["pmv65_sb"][:, :, 0:64],
                    in_=ap3(pm, 0, (64, 4), (256, 16), (1, 64))), 16)
                for dc in range(8):
                    S.inc("dg", g.dma_start(out=P["nwcols"][dc][:, :],
                                            in_=bass.AP(nw, 128 * dc, [[1, 128], [0, 1]])), 16)
                g.wait_ge(S.h["dg"], S.c["dg"])

        with ExitStack() as stB:
            sb2 = lambda n, sh, dt: stB.enter_context(nc.sbuf_tensor(_u(n), sh, dt))
            hmac = sb2("b_h", [128, 8, 2048], BF16)
            P["wqkv_sb"] = sb2("wqkv_sb", [128, 8, 3072], BF16)
            P["cos_sb"] = sb2("cos_sb", [128, 32, 32], F32)
            P["sin_sb"] = sb2("sin_sb", [128, 32, 32], F32)
            with nc.Block() as blk:
                @blk.gpsimd
                def _(g):
                    S.inc("dg", g.dma_start(
                        out=P["wqkv_sb"][:, :, :],
                        in_=ap3(wqkvT, 0, (3072, 128), (128 * 3072, 8), (1, 3072))), 16)
                    S.inc("dg", g.dma_start(out=P["cos_sb"][:, :, :], in_=cosr.ap()), 16)
                    S.inc("dg", g.dma_start(out=P["sin_sb"][:, :, :], in_=sinr.ap()), 16)
                    g.wait_ge(S.h["dg"], S.c["dg"])
            for m in range(2):
                emit_ln(nc, S, {"ones1": P["ones1"]}, xin, hmac, m, 4,
                        rms=dict(nw_cols=P["nwcols"]))
                emit_attn1_qkv(nc, S, P, hmac, kT_sc, v65_sc, 16 * m, 16)

        P["mT"] = sb("mT", [16, 512], BF16)
        P["sel16_sb"] = sb("sel16_sb", [16, 16, 128], BF16)
        with nc.Block() as blk:
            @blk.gpsimd
            def _(g):
                S.inc("dg", g.dma_start(out=P["sel16_sb"][:, :, :], in_=sel16.ap()), 16)
                g.wait_ge(S.h["dg"], S.c["dg"])

        P["qT"] = sb("qT", [128, 8, 512], BF16)
        P["kbuf"] = [sb("kbuf0", [128, 8, 512], BF16), sb("kbuf1", [128, 8, 512], BF16)]
        P["vbuf"] = [sb("vbuf0", [128, 4, 16, 65], BF16), sb("vbuf1", [128, 4, 16, 65], BF16)]
        P["oacc"] = sb("oacc", [65, 16, 512], F32)
        P["o_i"] = sb("o_i", [128, 8, 512], BF16)
        P["wout_sb"] = sb("wout_sb", [128, 8, 1024], BF16)
        with nc.Block() as blk:
            @blk.gpsimd
            def _(g):
                S.inc("dg", g.dma_start(
                    out=P["wout_sb"][:, :, :],
                    in_=ap3(woutT, 0, (D, 128), (128 * D, 8), (1, D))), 16)
                g.wait_ge(S.h["dg"], S.c["dg"])
        emit_attn1_core(nc, S, P, xin, xout, kT_sc, v65_sc, dbg=(dbg if debug else None))
        with nc.Block() as blk:
            @blk.sync
            def _(sy):
                sy.wait_ge(S.h["eps"], S.c["eps"])
    return nc



_CACHE = {}


def _consts():
    ident = np.eye(128, dtype=np.float32)
    sel = np.zeros((8, 9, 128), dtype=np.float32)
    for u in range(8):
        sel[u, u, :] = 1.0
    jl = np.arange(128)[:, None, None]
    r4 = np.arange(4)[None, :, None]
    il = np.arange(512)[None, None, :]
    masks = np.ascontiguousarray(((128 * r4 + jl) <= il).astype(np.float32))
    mh = np.zeros((32, 2, 128), np.float32)
    mh[0:16, 0, :] = 1.0
    mh[16:32, 1, :] = 1.0
    invf = 1.0 / (10000.0 ** (np.arange(0, 64, 2, dtype=np.float64) / 64))
    tpos = (np.arange(T).reshape(32, 128).T)[:, :, None] * invf[None, None, :]
    cosr = np.ascontiguousarray(np.cos(tpos).astype(np.float32))
    sinr = np.ascontiguousarray(np.sin(tpos).astype(np.float32))
    sel16 = np.zeros((16, 16, 128), np.float32)
    for hh in range(16):
        sel16[hh, hh, :] = 1.0
    return dict(ident=ident, sel=sel, masks=masks, mhalf=mh, cosr=cosr,
                sinr=sinr, sel16=sel16)


def _anp(a):
    return np.ascontiguousarray(np.asarray(a, np.float32))


def kernel(**inputs):
    C = _consts()
    if "ncs" not in _CACHE:
        _CACHE["ncs"] = dict(
            moe0=build_ffn_nc("moe0"), attn0=build_attn0_nc(),
            moe1=build_ffn_nc("moe1"), attn1=build_attn1_nc(),
        )
    ncs = _CACHE["ncs"]

    m_moe0 = dict(
        ident=C["ident"], sel=C["sel"],
        Wg=_anp(inputs["L0_ffn_Wg"]), Ws1=_anp(inputs["L0_ffn_Ws1"]),
        Ws3=_anp(inputs["L0_ffn_Ws3"]), Ws2=_anp(inputs["L0_ffn_Ws2"]),
        We1=_anp(inputs["L0_ffn_We1"]), We3=_anp(inputs["L0_ffn_We3"]),
        We2=_anp(inputs["L0_ffn_We2"]))
    gamma = _anp(inputs["L0_attn_gamma"])
    beta = _anp(inputs["L0_attn_beta"])
    gb = np.ascontiguousarray(np.stack([
        np.broadcast_to(np.tile(gamma, 16), (128, 256)),
        np.broadcast_to(np.tile(beta, 16), (128, 256))]).astype(np.float32))
    m_attn0 = dict(
        ident=C["ident"], gb=gb, masks=C["masks"], mhalf=C["mhalf"],
        WqT=_anp(np.asarray(inputs["L0_attn_Wq"]).T),
        WkT=_anp(np.asarray(inputs["L0_attn_Wk"]).T),
        WvT=_anp(np.asarray(inputs["L0_attn_Wv"]).T),
        WoT=_anp(np.asarray(inputs["L0_attn_Wo"]).T))
    m_moe1 = dict(
        ident=C["ident"], sel=C["sel"],
        wrT=_anp(np.asarray(inputs["L1_ffn_w_router"]).T),
        w1=_anp(inputs["L1_ffn_w1"]), w3=_anp(inputs["L1_ffn_w3"]),
        w2=_anp(inputs["L1_ffn_w2"]))
    pm = _anp(inputs["L1_attn_persist_mem"])
    m_attn1 = dict(
        ident=C["ident"], masks=C["masks"], cosr=C["cosr"], sinr=C["sinr"],
        sel16=C["sel16"],
        wqkvT=_anp(np.asarray(inputs["L1_attn_w_qkv"]).T),
        woutT=_anp(np.asarray(inputs["L1_attn_w_out"]).T),
        nw=_anp(inputs["L1_attn_norm_weight"]),
        pm=pm, pmT=np.ascontiguousarray(pm.transpose(0, 2, 1)))

    x = np.asarray(inputs["L0_attn_x"], np.float32)
    cur = [np.ascontiguousarray(x[c].T) for c in range(8)]

    for stage, wm in [("moe0", m_moe0), ("attn0", m_attn0), ("moe0", m_moe0),
                      ("moe1", m_moe1), ("attn1", m_attn1), ("moe1", m_moe1)]:
        in_maps = [dict(xin=cur[c], **wm) for c in range(8)]
        res = run_bass_kernel_spmd(ncs[stage], in_maps, core_ids=list(range(8)))
        cur = [np.ascontiguousarray(res.results[c]["xout"]) for c in range(8)]

    out = np.stack([cur[c].T for c in range(8)]).astype(np.float32)
    return out


# version 02
# speedup vs baseline: 1.0033x; 1.0033x over previous
import math
from contextlib import ExitStack

import numpy as np
import concourse.bass as bass
import concourse.mybir as mybir
from concourse.bass_utils import run_bass_kernel_spmd

F32 = mybir.dt.float32
BF16 = mybir.dt.bfloat16
AF = mybir.ActivationFunctionType
ALU = mybir.AluOpType
AX = mybir.AxisListType

T = 4096
D = 1024
LN_EPS = 1e-5
RMS_EPS2 = float(np.finfo(np.float32).eps)


class Sems:

    def __init__(self, nc, stack, names):
        self.h = {n: stack.enter_context(nc.semaphore(n)) for n in names}
        self.c = {n: 0 for n in names}

    def inc(self, name, instr, k=1):
        instr.then_inc(self.h[name], k)
        self.c[name] += k

    def base(self, *names):
        return {n: self.c[n] for n in names}

    def flush(self, eng, name, ins):
        ins.then_inc(self.h[name], 1)
        self.c[name] += 1
        eng.wait_ge(self.h[name], self.c[name])

    def xinc(self, eng, flname, ins, milestone, k=1):
        self.flush(eng, flname, ins)
        self.inc(milestone, eng.nop(), k)


_UID = [0]


def _u(name):
    _UID[0] += 1
    return f"{name}_{_UID[0]}"


def ap3(t, off, d0, d1, d2):
    return bass.AP(t, off, [list(d0), list(d1), list(d2)])


def xslice_ap(x_dram, tok, twid, ncols=T):
    return ap3(x_dram, tok, (ncols, 128), (128 * ncols, 8), (1, twid))



def emit_ln(nc, S, P, xin_dram, xn, m, n_t512, rms=None):
    ones1 = P["ones1"]
    t0 = m * n_t512 * 512
    r2 = 2 if rms is not None else 1
    b = S.base("lnx", "lnsq", "lnmm", "lnst", "lnv1", "lns1", "lnab", "lnk1",
               "lnbc", "lnap")

    with ExitStack() as ps:
      with nc.Block() as blk:
        sb = lambda n, sh, dt: ps.enter_context(nc.sbuf_tensor(_u(n), sh, dt))
        xf = sb("ln_xf", [128, 8, 512], F32)
        sq = sb("ln_sq", [128, 8, 512], F32)
        abbc = sb("ln_abbc", [128, 2, 512], F32)
        R = lambda k: sq[0:1, k, :]
        p_sum = ps.enter_context(nc.psum_tensor(_u("p_sum"), [1, 512], F32))
        p_sq = ps.enter_context(nc.psum_tensor(_u("p_sq"), [1, 512], F32))
        p_a = ps.enter_context(nc.psum_tensor(_u("p_a"), [128, 512], F32))
        p_b = ps.enter_context(nc.psum_tensor(_u("p_b"), [128, 512], F32))

        @blk.sync
        def _(sy):
            for q in range(n_t512):
                if q >= 1:
                    sy.wait_ge(S.h["lnap"], b["lnap"] + q)
                S.inc("lnx", sy.dma_start(out=xf[:, :, :],
                                          in_=xslice_ap(xin_dram, t0 + q * 512, 512)), 16)

        @blk.scalar
        def _(s):
            for q in range(n_t512):
                s.wait_ge(S.h["lnx"], b["lnx"] + 16 * (q + 1))
                if q >= 1:
                    s.wait_ge(S.h["lnap"], b["lnap"] + q)
                ins = None
                for dc in range(8):
                    ins = s.activation(out=sq[:, dc, :], in_=xf[:, dc, :], func=AF.Square)
                S.inc("lnsq", ins, 1)
                s.wait_ge(S.h["lnv1"], b["lnv1"] + q + 1)
                ins = s.activation(out=R(3), in_=R(2), func=AF.Sqrt)
                S.inc("lns1", ins, 1)
                if rms is not None:
                    s.wait_ge(S.h["lnab"], b["lnab"] + r2 * q + 1)
                    ins = s.activation(out=R(7), in_=R(6), func=AF.Sqrt)
                    S.inc("lns1", ins, 1)

        @blk.tensor
        def _(t):
            for q in range(n_t512):
                t.wait_ge(S.h["lnsq"], b["lnsq"] + q + 1)
                if b["lnst"] + q >= 1:
                    t.wait_ge(S.h["lnst"], b["lnst"] + q)
                ins = None
                for dc in range(8):
                    t.matmul(p_sum[:, :], ones1[0:128, 0:1], xf[:, dc, :],
                             start=(dc == 0), stop=(dc == 7))
                    ins = t.matmul(p_sq[:, :], ones1[0:128, 0:1], sq[:, dc, :],
                                   start=(dc == 0), stop=(dc == 7))
                S.inc("lnmm", ins, 1)
                t.wait_ge(S.h["lnab"], b["lnab"] + r2 * (q + 1))
                if b["lnbc"] + 2 * q >= 2:
                    t.wait_ge(S.h["lnbc"], b["lnbc"] + 2 * q)
                ins = t.matmul(p_a[:, :], ones1[0:1, 0:128], R(4),
                               start=True, stop=True)
                S.inc("lnk1", ins, 1)
                ins = t.matmul(p_b[:, :], ones1[0:1, 0:128], R(5),
                               start=True, stop=True)
                S.inc("lnk1", ins, 1)

        @blk.vector
        def _(v):
            FL = lambda ins: S.flush(v, "vfl", ins)
            for q in range(n_t512):
                v.wait_ge(S.h["lnmm"], b["lnmm"] + q + 1)
                v.tensor_single_scalar(R(0), p_sum[:, :], 1.0 / D, ALU.mult)
                ins = v.tensor_single_scalar(R(1), p_sq[:, :], 1.0 / D, ALU.mult)
                S.inc("lnst", ins, 1)
                v.tensor_tensor(R(2), R(0), R(0), ALU.mult)
                v.tensor_tensor(R(1), R(1), R(2), ALU.subtract)
                ins = v.tensor_single_scalar(R(2), R(1), LN_EPS, ALU.add)
                S.inc("lnv1", ins, 1)
                v.wait_ge(S.h["lns1"], b["lns1"] + r2 * q + 1)
                v.reciprocal(R(4), R(3))
                if rms is None:
                    v.tensor_tensor(R(5), R(0), R(4), ALU.mult)
                    ins = v.tensor_single_scalar(R(5), R(5), -1.0, ALU.mult)
                    S.inc("lnab", ins, 1)
                else:
                    v.tensor_tensor(R(6), R(1), R(4), ALU.mult)
                    v.tensor_tensor(R(6), R(6), R(4), ALU.mult)
                    ins = v.tensor_single_scalar(R(6), R(6), RMS_EPS2, ALU.add)
                    S.inc("lnab", ins, 1)
                    v.wait_ge(S.h["lns1"], b["lns1"] + 2 * (q + 1))
                    v.reciprocal(R(3), R(7))
                    v.tensor_tensor(R(4), R(4), R(3), ALU.mult)
                    v.tensor_tensor(R(5), R(0), R(4), ALU.mult)
                    ins = v.tensor_single_scalar(R(5), R(5), -1.0, ALU.mult)
                    S.inc("lnab", ins, 1)
                v.wait_ge(S.h["lnk1"], b["lnk1"] + 2 * (q + 1))
                ins = v.tensor_copy(abbc[:, 0, :], p_a[:, :])
                S.inc("lnbc", ins, 1)
                ins = v.tensor_copy(abbc[:, 1, :], p_b[:, :])
                S.inc("lnbc", ins, 1)
                ins = None
                for dc in range(8):
                    v.tensor_tensor(sq[:, dc, :], xf[:, dc, :], abbc[:, 0, :], ALU.mult)
                    if rms is None:
                        ins = v.tensor_tensor(xn[:, dc, q * 512:(q + 1) * 512],
                                              sq[:, dc, :], abbc[:, 1, :], ALU.add)
                    else:
                        v.tensor_tensor(sq[:, dc, :], sq[:, dc, :], abbc[:, 1, :], ALU.add)
                        ins = v.tensor_single_scalar(xn[:, dc, q * 512:(q + 1) * 512],
                                                     sq[:, dc, :],
                                                     rms["nw_cols"][dc][:, :], ALU.mult)
                S.inc("lnap", ins, 1)



def emit_gates(nc, S, P, fam, xn, m, n_t512, dbg=None):
    w8, wg_sb, idf = P["w8"], P["wg_sb"], P["idf"]
    nC = n_t512 * 4
    soft = fam["gate"] == "soft"
    b = S.base("gmm", "gsg", "gw", "gtr", "gtw", "gvm")

    with ExitStack() as ps:
      with nc.Block() as blk:
        sb = lambda n, sh, dt: ps.enter_context(nc.sbuf_tensor(_u(n), sh, dt))
        gtok = sb("g_gtok", [128, 8], F32)
        wtok = sb("g_wtok", [128, 8], F32)
        msk = sb("g_msk", [128, 8], F32)
        zer = sb("g_zer", [128, 8], F32)
        c_m1 = sb("g_m1", [128, 1], F32)
        c_m2 = sb("g_m2", [128, 1], F32)
        c_den = sb("g_den", [128, 1], F32)
        c_rcp = sb("g_rcp", [128, 1], F32)
        c_neg = sb("g_neg", [128, 1], F32)
        p_g = ps.enter_context(nc.psum_tensor(_u("p_g"), [128, 8], F32))
        p_wt = ps.enter_context(nc.psum_tensor(_u("p_wt"), [8, 128], F32))

        @blk.tensor
        def _(t):
            for c in range(nC):
                if b["gsg"] + c >= 1:
                    t.wait_ge(S.h["gsg"], b["gsg"] + c)
                ins = None
                for K in range(8):
                    ins = t.matmul(p_g[:, :], xn[:, K, c * 128:(c + 1) * 128],
                                   wg_sb[:, K, :], start=(K == 0), stop=(K == 7))
                S.inc("gmm", ins, 1)
                t.wait_ge(S.h["gw"], b["gw"] + c + 1)
                if b["gtw"] + c >= 1:
                    t.wait_ge(S.h["gtw"], b["gtw"] + c)
                ins = t.transpose(p_wt[:, :], wtok[:, :], idf[:, :])
                S.inc("gtr", ins, 1)

        @blk.scalar
        def _(s):
            for c in range(nC):
                if b["gw"] + c >= 1:
                    s.wait_ge(S.h["gw"], b["gw"] + c)
                if soft:
                    s.wait_ge(S.h["gvm"], b["gvm"] + c + 1)
                    ins = s.activation(out=gtok[:, :], in_=p_g[:, :], func=AF.Exp,
                                       bias=c_neg[:, :], scale=1.0)
                else:
                    s.wait_ge(S.h["gmm"], b["gmm"] + c + 1)
                    ins = s.activation(out=gtok[:, :], in_=p_g[:, :], func=AF.Sigmoid)
                S.inc("gsg", ins, 1)

        loc = dict(gtok=gtok, wtok=wtok, c_m1=c_m1, c_m2=c_m2,
                   c_rcp=c_rcp, c_den=c_den, zer=zer)

        @blk.vector
        def _(v):
            v.memset(zer[:, :], 0.0)
            for c in range(nC):
                if soft:
                    v.wait_ge(S.h["gmm"], b["gmm"] + c + 1)
                    S.flush(v, "vfl", v.reduce_max(c_m1[:, :], p_g[:, :], AX.X))
                    ins = v.tensor_single_scalar(c_neg[:, :], c_m1[:, :], -1.0, ALU.mult)
                    S.inc("gvm", ins, 1)
                v.wait_ge(S.h["gsg"], b["gsg"] + c + 1)
                if b["gtr"] + c >= 1:
                    v.wait_ge(S.h["gtr"], b["gtr"] + c)
                FL = lambda ins: S.flush(v, "vfl", ins)
                if soft:
                    FL(v.reduce_sum(c_den[:, :], gtok[:, :], AX.X))
                    FL(v.reciprocal(c_rcp[:, :], c_den[:, :]))
                    FL(v.tensor_tensor(gtok[:, :], gtok[:, :],
                                       c_rcp[:, 0:1].broadcast_to((128, 8)), ALU.mult))
                FL(v.reduce_max(c_m1[:, :], gtok[:, :], AX.X))
                FL(v.tensor_tensor(msk[:, :], gtok[:, :],
                                   c_m1[:, 0:1].broadcast_to((128, 8)), ALU.is_lt))
                FL(v.tensor_tensor(wtok[:, :], gtok[:, :], msk[:, :], ALU.mult))
                FL(v.reduce_max(c_m2[:, :], wtok[:, :], AX.X))
                FL(v.tensor_tensor(msk[:, :], gtok[:, :],
                                   c_m2[:, 0:1].broadcast_to((128, 8)), ALU.is_ge))
                FL(v.tensor_tensor(wtok[:, :], gtok[:, :], msk[:, :], ALU.mult))
                if not soft:
                    FL(v.tensor_tensor(c_den[:, :], c_m1[:, :], c_m2[:, :], ALU.add))
                    FL(v.reciprocal(c_rcp[:, :], c_den[:, :]))
                    FL(v.tensor_tensor(wtok[:, :], wtok[:, :],
                                       c_rcp[:, 0:1].broadcast_to((128, 8)), ALU.mult))
                ins = v.tensor_copy(msk[:, :], wtok[:, :])
                S.inc("gw", ins, 1)
                v.wait_ge(S.h["gtr"], b["gtr"] + c + 1)
                ins = v.tensor_copy(w8[0:8, c * 128:(c + 1) * 128], p_wt[0:8, :])
                S.inc("gtw", ins, 1)

      if dbg is not None:
        for k, t in loc.items():
            _dbg_dump(nc, S, t, dbg[k])



def emit_experts(nc, S, P, fam, xn, m, n_t512):
    w8, sel_sb, accf, ones1, onesbc = P["w8"], P["sel_sb"], P["accf"], P["ones1"], P["onesbc"]
    nT = n_t512
    units = fam["units"]
    granules = fam["granules"]

    b = S.base("dw", "wbck1", "wbcev", "h1g", "h3g", "hsl", "h3w", "hmul",
               "dnmm", "accev")
    sched = []
    cnt = dict(dw=0, wbck1=0, wbcev=0, h1g=0, h3g=0, hsl=0, h3w=0, hmul=0,
               dnmm=0, accev=0)
    fi_glob = 0
    for u, unit in enumerate(units):
        for g, (f0, fg) in enumerate(granules):
            nfi = fg // 128
            sched.append(dict(u=u, g=g, f0=f0, fg=fg, nfi=nfi, gi=len(sched),
                              unit=unit, bc=dict(cnt), fi0=fi_glob))
            cnt["dw"] += 48
            if g == 0 and unit["gated"]:
                cnt["wbck1"] += nT
                cnt["wbcev"] += nT
            cnt["h1g"] += 4 * nfi
            cnt["h3g"] += 4 * nfi
            cnt["hsl"] += 4 * nfi
            cnt["h3w"] += 4 * nfi
            cnt["hmul"] += 4 * nfi
            cnt["dnmm"] += nT * 8
            cnt["accev"] += nT * 8
            fi_glob += nfi

    with ExitStack() as ps, nc.Block() as blk:
        sb = lambda n, sh, dt: ps.enter_context(nc.sbuf_tensor(_u(n), sh, dt))
        w1g = [sb("w1g0", [128, 8, 512], BF16), sb("w1g1", [128, 8, 512], BF16)]
        w3g = [sb("w3g0", [128, 8, 512], BF16), sb("w3g1", [128, 8, 512], BF16)]
        w2g = [sb("w2g0", [128, 4, 1024], BF16), sb("w2g1", [128, 4, 1024], BF16)]
        hsb = sb("hsb", [128, 4, 4, 512], BF16)
        silu = sb("silu", [128, 2, 4, 512], BF16)
        wbc = sb("wbc", [128, 4, 512], F32)
        ph1 = ps.enter_context(nc.psum_tensor(_u("ph1"), [128, 4, 512], F32))
        ph3 = ps.enter_context(nc.psum_tensor(_u("ph3"), [128, 4, 512], F32))

        @blk.gpsimd
        def _(gp):
            for ent in sched:
                gi = ent["gi"]
                if gi >= 2:
                    gi2 = sched[gi - 2]
                    gp.wait_ge(S.h["dnmm"], b["dnmm"] + gi2["bc"]["dnmm"] + nT * 8)
                buf = gi % 2
                un = ent["unit"]
                f0, fg, nfi = ent["f0"], ent["fg"], ent["nfi"]
                Fd = un["F"]
                S.inc("dw", gp.dma_start(
                    out=w1g[buf][:, :, 0:fg],
                    in_=ap3(un["w1t"], un["w1off"] + f0, (Fd, 128), (128 * Fd, 8), (1, fg))), 16)
                S.inc("dw", gp.dma_start(
                    out=w3g[buf][:, :, 0:fg],
                    in_=ap3(un["w3t"], un["w3off"] + f0, (Fd, 128), (128 * Fd, 8), (1, fg))), 16)
                S.inc("dw", gp.dma_start(
                    out=w2g[buf][:, 0:nfi, :],
                    in_=ap3(un["w2t"], un["w2off"] + f0 * D, (D, 128), (128 * D, nfi), (1, D))), 16)

        @blk.tensor
        def _(t):
            for ent in sched:
                gi, u, g, fg, nfi = ent["gi"], ent["u"], ent["g"], ent["fg"], ent["nfi"]
                bc = ent["bc"]
                buf = gi % 2
                gated = ent["unit"]["gated"]
                t.wait_ge(S.h["dw"], b["dw"] + bc["dw"] + 48)
                if g == 0 and gated:
                    for tt in range(nT):
                        if bc["wbcev"] + tt >= 1:
                            t.wait_ge(S.h["wbcev"], b["wbcev"] + bc["wbcev"] + tt)
                        if bc["h3w"] >= 1:
                            t.wait_ge(S.h["h3w"], b["h3w"] + bc["h3w"])
                        ins = t.matmul(ph3[:, 0, :], sel_sb[0:8, u, :],
                                       w8[0:8, tt * 512:(tt + 1) * 512],
                                       start=True, stop=True)
                        S.inc("wbck1", ins, 1)
                for fi in range(nfi):
                    fig = ent["fi0"] + fi
                    if b["hsl"] + 4 * fig >= 4:
                        t.wait_ge(S.h["hsl"], b["hsl"] + 4 * fig)
                    if fi == 0 and bc["accev"] >= 1:
                        t.wait_ge(S.h["accev"], b["accev"] + bc["accev"])
                    for K in range(8):
                        for tt in range(nT):
                            ins = t.matmul(ph1[:, tt, :],
                                           w1g[buf][:, K, fi * 128:(fi + 1) * 128],
                                           xn[:, K, tt * 512:(tt + 1) * 512],
                                           start=(K == 0), stop=(K == 7))
                            if K == 7:
                                S.inc("h1g", ins, 1)
                    if b["h3w"] + 4 * fig >= 4:
                        t.wait_ge(S.h["h3w"], b["h3w"] + 4 * fig)
                    if g == 0 and fi == 0 and gated:
                        t.wait_ge(S.h["wbcev"], b["wbcev"] + bc["wbcev"] + nT)
                    for K in range(8):
                        for tt in range(nT):
                            ins = t.matmul(ph3[:, tt, :],
                                           w3g[buf][:, K, fi * 128:(fi + 1) * 128],
                                           xn[:, K, tt * 512:(tt + 1) * 512],
                                           start=(K == 0), stop=(K == 7))
                            if K == 7:
                                S.inc("h3g", ins, 1)
                t.wait_ge(S.h["hsl"], b["hsl"] + bc["hsl"] + 4 * nfi)
                t.wait_ge(S.h["hmul"], b["hmul"] + bc["hmul"] + 4 * nfi)
                for tt in range(nT):
                    for d in range(8):
                        need = bc["dnmm"] + 8 * tt + d - 1
                        if need >= 1:
                            t.wait_ge(S.h["accev"], b["accev"] + need)
                        for fi in range(nfi):
                            ins = t.matmul(ph1[:, d % 2, :],
                                           w2g[buf][:, fi, d * 128:(d + 1) * 128],
                                           hsb[:, fi, tt, :],
                                           start=(fi == 0), stop=(fi == nfi - 1))
                        S.inc("dnmm", ins, 1)

        @blk.scalar
        def _(s):
            for ent in sched:
                for fi in range(ent["nfi"]):
                    fig = ent["fi0"] + fi
                    if fig >= 2:
                        s.wait_ge(S.h["hmul"], b["hmul"] + 4 * (fig - 1))
                    for tt in range(nT):
                        s.wait_ge(S.h["h1g"], b["h1g"] + 4 * fig + tt + 1)
                        ins = s.activation(out=silu[:, fig % 2, tt, :], in_=ph1[:, tt, :],
                                           func=AF.Silu)
                        S.inc("hsl", ins, 1)

        @blk.vector
        def _(v):
            for ent in sched:
                gi, g, nfi = ent["gi"], ent["g"], ent["nfi"]
                bc = ent["bc"]
                gated = ent["unit"]["gated"]
                wsrc = wbc if gated else onesbc
                if g == 0 and gated:
                    for tt in range(nT):
                        v.wait_ge(S.h["wbck1"], b["wbck1"] + bc["wbck1"] + tt + 1)
                        ins = v.tensor_copy(wbc[:, tt, :], ph3[:, 0, :])
                        S.inc("wbcev", ins, 1)
                for fi in range(nfi):
                    fig = ent["fi0"] + fi
                    for tt in range(nT):
                        v.wait_ge(S.h["h3g"], b["h3g"] + 4 * fig + tt + 1)
                        win = wsrc[:, tt, :] if gated else onesbc[:, :]
                        ins = v.scalar_tensor_tensor(hsb[:, fi, tt, :], ph3[:, tt, :], 1.0,
                                                     win, ALU.mult, ALU.mult)
                        S.inc("h3w", ins, 1)
                        v.wait_ge(S.h["hsl"], b["hsl"] + 4 * fig + tt + 1)
                        ins = v.tensor_tensor(hsb[:, fi, tt, :], silu[:, fig % 2, tt, :],
                                              hsb[:, fi, tt, :], ALU.mult)
                        S.inc("hmul", ins, 1)
                for tt in range(nT):
                    for d in range(8):
                        v.wait_ge(S.h["dnmm"], b["dnmm"] + bc["dnmm"] + 8 * tt + d + 1)
                        dst = accf[:, d, tt * 512:(tt + 1) * 512]
                        if ent["u"] == 0 and g == 0:
                            ins = v.tensor_copy(dst, ph1[:, d % 2, :])
                        else:
                            ins = v.scalar_tensor_tensor(dst, ph1[:, d % 2, :], 1.0,
                                                         dst, ALU.mult, ALU.add)
                        S.inc("accev", ins, 1)



def emit_epilogue(nc, S, xin_dram, xout_dram, accf, scale, m, n_t512):
    t0 = m * n_t512 * 512
    b = S.base("epx", "epv", "eps")
    with ExitStack() as ps, nc.Block() as blk:
        sb = lambda n, sh, dt: ps.enter_context(nc.sbuf_tensor(_u(n), sh, dt))
        xf = sb("ep_xf", [128, 8, 512], F32)
        outst = sb("ep_out", [128, 8, 512], F32)

        @blk.sync
        def _(sy):
            for q in range(n_t512):
                if q >= 1:
                    sy.wait_ge(S.h["epv"], b["epv"] + q)
                S.inc("epx", sy.dma_start(out=xf[:, :, :],
                                          in_=xslice_ap(xin_dram, t0 + q * 512, 512)), 16)
                sy.wait_ge(S.h["epv"], b["epv"] + q + 1)
                S.inc("eps", sy.dma_start(
                    out=xslice_ap(xout_dram, t0 + q * 512, 512),
                    in_=outst[:, :, :]), 16)

        @blk.vector
        def _(v):
            for q in range(n_t512):
                v.wait_ge(S.h["epx"], b["epx"] + 16 * (q + 1))
                if q >= 1:
                    v.wait_ge(S.h["eps"], b["eps"] + 16 * q)
                ins = None
                for dc in range(8):
                    ins = v.scalar_tensor_tensor(outst[:, dc, :],
                                                 accf[:, dc, q * 512:(q + 1) * 512], scale,
                                                 xf[:, dc, :], ALU.mult, ALU.add)
                S.inc("epv", ins, 1)



FFN_SEMS = ["lnx", "lnsq", "lnmm", "lnst", "lnv1", "lns1", "lnab", "lnk1", "lnbc",
            "lnap", "gmm", "gsg", "gw", "gtr", "gtw", "gvm", "dg", "dw", "vfl", "wbck1",
            "wbcev", "h1g", "h3g", "hsl", "h3w", "hmul", "dnmm", "accev",
            "epx", "epv", "eps"]


def make_ffn_fam(nc, kind):
    if kind == "moe0":
        Wg = nc.dram_tensor("Wg", [D, 8], F32, kind="ExternalInput")
        Ws1 = nc.dram_tensor("Ws1", [D, 2048], F32, kind="ExternalInput")
        Ws3 = nc.dram_tensor("Ws3", [D, 2048], F32, kind="ExternalInput")
        Ws2 = nc.dram_tensor("Ws2", [2048, D], F32, kind="ExternalInput")
        We1 = nc.dram_tensor("We1", [8, D, 2048], F32, kind="ExternalInput")
        We3 = nc.dram_tensor("We3", [8, D, 2048], F32, kind="ExternalInput")
        We2 = nc.dram_tensor("We2", [8, 2048, D], F32, kind="ExternalInput")
        units = []
        for e in range(8):
            units.append(dict(F=2048, slot=e, gated=True,
                              w1t=We1, w1off=e * D * 2048, w3t=We3, w3off=e * D * 2048,
                              w2t=We2, w2off=e * 2048 * D))
        units.append(dict(F=2048, slot=8, gated=False,
                          w1t=Ws1, w1off=0, w3t=Ws3, w3off=0, w2t=Ws2, w2off=0))
        return dict(kind=kind, gate="sig", scale=1.0 / 6.0, wg=Wg, units=units,
                    granules=[(0, 512), (512, 512), (1024, 512), (1536, 512)])
    else:
        wrT = nc.dram_tensor("wrT", [D, 8], F32, kind="ExternalInput")
        w1 = nc.dram_tensor("w1", [8, D, 1408], F32, kind="ExternalInput")
        w3 = nc.dram_tensor("w3", [8, D, 1408], F32, kind="ExternalInput")
        w2 = nc.dram_tensor("w2", [8, 1408, D], F32, kind="ExternalInput")
        units = []
        for e in range(8):
            units.append(dict(F=1408, slot=e, gated=True,
                              w1t=w1, w1off=e * D * 1408, w3t=w3, w3off=e * D * 1408,
                              w2t=w2, w2off=e * 1408 * D))
        return dict(kind=kind, gate="soft", scale=0.5, wg=wrT, units=units,
                    granules=[(0, 512), (512, 512), (1024, 384)])


def alloc_persistent(nc, st):
    sb = lambda n, sh, dt: st.enter_context(nc.sbuf_tensor(_u(n), sh, dt))
    return dict(
        ones1=sb("ones1", [128, 128], F32),
        idf=sb("idf", [128, 128], F32),
        onesbc=sb("onesbc", [128, 512], F32),
        xn=sb("xn", [128, 8, 2048], BF16),
        accf=sb("accf", [128, 8, 2048], F32),
        w8=sb("w8", [8, 2048], F32),
        sel_sb=sb("sel_sb", [8, 9, 128], F32),
        wg_sb=sb("wg_sb", [128, 8, 8], BF16),
    )


def emit_ffn_init(nc, S, P, fam, ident, sel):
    with nc.Block() as blk:
        @blk.gpsimd
        def _(g):
            g.memset(P["ones1"][:, :], 1.0)
            g.memset(P["onesbc"][:, :], 1.0)
            S.inc("dg", g.dma_start(out=P["idf"][:, :], in_=ident.ap()), 16)
            S.inc("dg", g.dma_start(out=P["sel_sb"][:, :, :], in_=sel.ap()), 16)
            S.inc("dg", g.dma_start(
                out=P["wg_sb"][:, :, :],
                in_=ap3(fam["wg"], 0, (8, 128), (128 * 8, 8), (1, 8))), 16)
            g.wait_ge(S.h["dg"], S.c["dg"])


def emit_ffn_stage(nc, S, Dh, kind, xin, xout, debug=False, dbg_t=None):
    fam = Dh["fam_" + kind]
    with ExitStack() as st:
        sb = lambda n, sh, dt: st.enter_context(nc.sbuf_tensor(_u(n), sh, dt))
        P = dict(
            ones1=sb("ones1", [128, 128], F32),
            idf=sb("idf", [128, 128], F32),
            onesbc=sb("onesbc", [128, 512], F32),
            xn=sb("xn", [128, 8, 2048], BF16),
            accf=sb("accf", [128, 8, 2048], F32),
            w8=sb("w8", [8, 2048], F32),
            sel_sb=sb("sel_sb", [8, 9, 128], F32),
            wg_sb=sb("wg_sb", [128, 8, 8], BF16),
        )
        emit_ffn_init(nc, S, P, fam, Dh["ident"], Dh["sel"])
        for m in range(2):
            emit_ln(nc, S, P, xin, P["xn"], m, 4)
            emit_gates(nc, S, P, fam, P["xn"], m, 4)
            emit_experts(nc, S, P, fam, P["xn"], m, 4)
            emit_epilogue(nc, S, xin, xout, P["accf"], fam["scale"], m, 4)


def build_ffn_nc(kind, debug=False):
    nc = bass.Bass("TRN2", target_bir_lowering=False, detect_race_conditions=False)
    xin = nc.dram_tensor("xin", [D, T], F32, kind="ExternalInput")
    ident = nc.dram_tensor("ident", [128, 128], F32, kind="ExternalInput")
    sel = nc.dram_tensor("sel", [8, 9, 128], F32, kind="ExternalInput")
    xout = nc.dram_tensor("xout", [D, T], F32, kind="ExternalOutput")
    Dh = dict(ident=ident, sel=sel)
    Dh["fam_" + kind] = make_ffn_fam(nc, kind)
    with ExitStack() as stS:
        S = Sems(nc, stS, ALL_SEMS)
        emit_ffn_stage(nc, S, Dh, kind, xin, xout)
        with nc.Block() as blk:
            @blk.sync
            def _(sy):
                sy.wait_ge(S.h["eps"], S.c["eps"])
    return nc



ATTN0_SEMS = ["sfl", "aq_mm", "aq_ev", "av1", "as1", "atr_mm", "atr_ev", "a_smm",
              "a_sev", "a_dg", "a_omm", "a_zr", "a_zk", "a_zbev", "a_oev",
              "a_wo"]


def emit_attn0_qkv(nc, S, P, xn, c0=0, nc_chunks=32, dbg=None):
    b = S.base("aq_mm", "aq_ev", "av1", "as1")
    with ExitStack() as ps:
      with nc.Block() as blk:
        sb = lambda n, sh, dt: ps.enter_context(nc.sbuf_tensor(_u(n), sh, dt))
        qd = sb("a_qd", [128, 2, 256], F32)
        sqs = sb("a_sqs", [128, 256], F32)
        m16 = sb("a_m16", [128, 16], F32)
        v16 = sb("a_v16", [128, 16], F32)
        srt = sb("a_srt", [128, 16], F32)
        rsd = sb("a_rsd", [128, 16], F32)
        t1 = sb("a_t1", [128, 256], F32)
        pq = [ps.enter_context(nc.psum_tensor(_u("a_pq"), [128, 256], F32)) for _ in range(2)]
        pk = [ps.enter_context(nc.psum_tensor(_u("a_pk"), [128, 256], F32)) for _ in range(2)]
        pv = [ps.enter_context(nc.psum_tensor(_u("a_pv"), [128, 1024], F32)) for _ in range(2)]

        @blk.tensor
        def _(t):
            for cc in range(nc_chunks):
                pr = cc % 2
                if cc >= 2:
                    t.wait_ge(S.h["aq_ev"], b["aq_ev"] + cc - 1)
                ins = None
                for K in range(8):
                    st_ = (K == 0)
                    sp = (K == 7)
                    xl = xn[:, K, cc * 128:(cc + 1) * 128]
                    t.matmul(pq[pr][:, :], xl, P["wq_sb"][:, K, :], start=st_, stop=sp)
                    t.matmul(pk[pr][:, :], xl, P["wk_sb"][:, K, :], start=st_, stop=sp)
                    t.matmul(pv[pr][:, 0:512], xl, P["wv_sb"][:, K, 0:512], start=st_, stop=sp)
                    ins = t.matmul(pv[pr][:, 512:1024], xl, P["wv_sb"][:, K, 512:1024],
                                   start=st_, stop=sp)
                S.inc("aq_mm", ins, 1)

        @blk.scalar
        def _(s):
            for cc in range(nc_chunks):
                for x2 in range(2):
                    s.wait_ge(S.h["av1"], b["av1"] + 4 * cc + 2 * x2 + 1)
                    ins = s.activation(out=sqs[:, :], in_=qd[:, x2, :], func=AF.Square)
                    S.xinc(s, "sfl", ins, "as1")
                    s.wait_ge(S.h["av1"], b["av1"] + 4 * cc + 2 * x2 + 2)
                    ins = s.activation(out=srt[:, :], in_=v16[:, :], func=AF.Sqrt)
                    S.xinc(s, "sfl", ins, "as1")

        @blk.vector
        def _(v):
            FL = lambda ins: S.flush(v, "vfl", ins)
            for cc in range(nc_chunks):
                c = c0 + cc
                pr = cc % 2
                v.wait_ge(S.h["aq_mm"], b["aq_mm"] + cc + 1)
                for x2 in range(2):
                    pp = pq[pr] if x2 == 0 else pk[pr]
                    FL(v.tensor_reduce(m16[:, :],
                                       pp[:, :].rearrange("p (h f) -> p h f", h=16),
                                       AX.X, ALU.add))
                    FL(v.tensor_single_scalar(m16[:, :], m16[:, :], 1.0 / 16, ALU.mult))
                    ins = v.tensor_tensor(qd[:, x2, :].rearrange("p (h f) -> p h f", h=16),
                                          pp[:, :].rearrange("p (h f) -> p h f", h=16),
                                          m16[:, :].rearrange("p (h o) -> p h o", o=1)
                                          .broadcast_to((128, 16, 16)), ALU.subtract)
                    S.xinc(v, "vfl", ins, "av1")
                    v.wait_ge(S.h["as1"], b["as1"] + 4 * cc + 2 * x2 + 1)
                    FL(v.tensor_reduce(v16[:, :],
                                       sqs[:, :].rearrange("p (h f) -> p h f", h=16),
                                       AX.X, ALU.add))
                    FL(v.tensor_single_scalar(v16[:, :], v16[:, :], 1.0 / 16, ALU.mult))
                    ins = v.tensor_single_scalar(v16[:, :], v16[:, :], LN_EPS, ALU.add)
                    S.xinc(v, "vfl", ins, "av1")
                    v.wait_ge(S.h["as1"], b["as1"] + 4 * cc + 2 * x2 + 2)
                    FL(v.reciprocal(rsd[:, :], srt[:, :]))
                    v.tensor_tensor(t1[:, :].rearrange("p (h f) -> p h f", h=16),
                                    qd[:, x2, :].rearrange("p (h f) -> p h f", h=16),
                                    rsd[:, :].rearrange("p (h o) -> p h o", o=1)
                                    .broadcast_to((128, 16, 16)), ALU.mult)
                    gsrc = P["gq_bc"] if x2 == 0 else P["gk_bc"]
                    bsrc = P["bq_bc"] if x2 == 0 else P["bk_bc"]
                    dtok = P["qtok"] if x2 == 0 else P["ktok"]
                    v.tensor_tensor(t1[:, :], t1[:, :], gsrc[:, :], ALU.mult)
                    v.tensor_tensor(dtok[:, c, :], t1[:, :], bsrc[:, :], ALU.add)
                ins = v.tensor_copy(
                    P["v65"][:, c, :, 0:64],
                    pv[pr][:, :].rearrange("p (h f) -> p h f", h=16))
                S.inc("aq_ev", ins, 1)

      if dbg is not None:
        for nm, tt_ in (("m16", m16), ("v16", v16), ("rsd", rsd), ("srt", srt),
                        ("qd", qd), ("t1", t1)):
            if nm in dbg:
                _dbg_dump(nc, S, tt_, dbg[nm])


def emit_attn0_tr(nc, S, P):
    b = S.base("atr_mm", "atr_ev")
    with ExitStack() as ps:
      with nc.Block() as blk:
        pt = [ps.enter_context(nc.psum_tensor(_u("a_pt"), [32, 128], BF16)) for _ in range(2)]

        @blk.tensor
        def _(t):
            idx = 0
            for c in range(32):
                for kind in range(2):
                    for p in range(8):
                        if idx >= 2:
                            t.wait_ge(S.h["atr_ev"], b["atr_ev"] + idx - 1)
                        src = P["qtok"] if kind == 0 else P["ktok"]
                        ins = t.transpose(pt[idx % 2][:, :],
                                          src[:, c, 32 * p:32 * p + 32], P["idb"][:, :])
                        S.inc("atr_mm", ins, 1)
                        idx += 1

        @blk.vector
        def _(v):
            idx = 0
            for c in range(32):
                for kind in range(2):
                    for p in range(8):
                        v.wait_ge(S.h["atr_mm"], b["atr_mm"] + idx + 1)
                        base = 32 * (p % 4)
                        tn = p // 4
                        if kind == 0:
                            for hi in range(2):
                                ins = v.tensor_tensor(
                                    P["qfT"][tn][base:base + 32, hi, c * 128:(c + 1) * 128],
                                    pt[idx % 2][:, :], P["mhalf"][0:32, hi, :], ALU.mult)
                        else:
                            ins = v.tensor_copy(
                                P["kfT"][tn][base:base + 32, c * 128:(c + 1) * 128],
                                pt[idx % 2][:, :])
                        S.inc("atr_ev", ins, 1)
                        idx += 1


def emit_attn0_core(nc, S, P, xin_dram, xout_dram, dbg=None):
    b = S.base("a_smm", "a_sev", "a_dg", "a_omm", "a_zr", "a_zk", "a_zbev",
               "a_oev", "a_wo", "epx", "epv", "eps")
    ents = []
    smm = sev = dg = omm = 0
    for i in range(8):
        for h in range(16):
            njc = 4 * i + 4
            tiles = []
            for jc in range(njc):
                diag = jc >= 4 * i
                tiles.append(dict(jc=jc, diag=diag, smm=smm, sev=sev, dg=dg, omm=omm + jc))
                smm += 1
                sev += 1
                if diag:
                    dg += 1
            ents.append(dict(i=i, h=h, njc=njc, tiles=tiles, ei=i * 16 + h, om0=omm))
            omm += njc

    with ExitStack() as ps:
      with nc.Block() as blk:
        sb = lambda n, sh, dt: ps.enter_context(nc.sbuf_tensor(_u(n), sh, dt))
        ssq = sb("a_ssq", [128, 3, 512], BF16)
        o_i = sb("a_oi", [128, 8, 512], BF16)
        zrec = sb("a_zrec", [1, 512], F32)
        zb = sb("a_zb", [64, 512], F32)
        xf = sb("a_xf", [128, 8, 512], F32)
        outst = sb("a_outst", [128, 8, 512], F32)
        ps_s = [ps.enter_context(nc.psum_tensor(_u("a_ps"), [128, 512], F32)) for _ in range(3)]
        ps_o = ps.enter_context(nc.psum_tensor(_u("a_po"), [128, 512], F32))
        pz = ps.enter_context(nc.psum_tensor(_u("a_pz"), [64, 512], F32))
        py = [ps.enter_context(nc.psum_tensor(_u("a_py"), [128, 512], F32)) for _ in range(2)]

        def o_wait_emit(t, ent, k):
            te = ent["tiles"][k]
            if te["diag"]:
                t.wait_ge(S.h["a_dg"], b["a_dg"] + te["dg"] + 1)
            else:
                t.wait_ge(S.h["a_sev"], b["a_sev"] + te["sev"] + 1)
            if k == 0:
                if ent["ei"] >= 1:
                    t.wait_ge(S.h["a_oev"], b["a_oev"] + ent["ei"])
            h = ent["h"]
            ins = t.matmul(ps_o[0:65, :],
                           P["v65"][:, te["jc"], h, 0:65],
                           ssq[:, te["sev"] % 3, :],
                           start=(k == 0), stop=(k == ent["njc"] - 1))
            S.inc("a_omm", ins, 1)

        @blk.tensor
        def _(t):
            for ent in ents:
                i, h = ent["i"], ent["h"]
                p = h // 2
                hi = h % 2
                base = 32 * (p % 4)
                tn = p // 4
                for jc in range(ent["njc"]):
                    te = ent["tiles"][jc]
                    if te["sev"] >= 3:
                        t.wait_ge(S.h["a_sev"], b["a_sev"] + te["sev"] - 2)
                    ins = t.matmul(ps_s[te["sev"] % 3][:, :],
                                   P["kfT"][tn][base:base + 32, jc * 128:(jc + 1) * 128],
                                   P["qfT"][tn][base:base + 32, hi, i * 512:(i + 1) * 512],
                                   start=True, stop=True, tile_position=(base, 0))
                    S.inc("a_smm", ins, 1)
                    if jc >= 2:
                        o_wait_emit(t, ent, jc - 2)
                o_wait_emit(t, ent, ent["njc"] - 2)
                o_wait_emit(t, ent, ent["njc"] - 1)
                t.wait_ge(S.h["a_zr"], b["a_zr"] + ent["ei"] + 1)
                if b["a_zbev"] + ent["ei"] >= 1:
                    t.wait_ge(S.h["a_zbev"], b["a_zbev"] + ent["ei"])
                ins = t.matmul(pz[:, :], P["ones1"][0:1, 0:64], zrec[0:1, :],
                               start=True, stop=True)
                S.inc("a_zk", ins, 1)
                if h == 15:
                    t.wait_ge(S.h["a_oev"], b["a_oev"] + 16 * (i + 1))
                    for m in range(8):
                        gm = 8 * i + m
                        if gm >= 2:
                            t.wait_ge(S.h["epv"], b["epv"] + gm - 1)
                        ins = None
                        for oc in range(8):
                            ins = t.matmul(py[m % 2][:, :],
                                           P["wo_sb"][:, oc, m * 128:(m + 1) * 128],
                                           o_i[:, oc, :],
                                           start=(oc == 0), stop=(oc == 7))
                        S.inc("a_wo", ins, 1)

        @blk.scalar
        def _(s):
            for ent in ents:
                for te in ent["tiles"]:
                    s.wait_ge(S.h["a_smm"], b["a_smm"] + te["smm"] + 1)
                    if te["sev"] >= 3:
                        s.wait_ge(S.h["a_omm"], b["a_omm"] + omm_of_sev(ents, te["sev"] - 3) + 1)
                    ins = s.activation(out=ssq[:, te["sev"] % 3, :],
                                       in_=ps_s[te["sev"] % 3][:, :], func=AF.Square)
                    S.inc("a_sev", ins, 1)

        @blk.vector
        def _(v):
            FL = lambda ins: S.flush(v, "vfl", ins)
            for ent in ents:
                i, h = ent["i"], ent["h"]
                if h == 0 and i >= 1:
                    v.wait_ge(S.h["a_wo"], b["a_wo"] + 8 * i)
                for te in ent["tiles"]:
                    if te["diag"]:
                        v.wait_ge(S.h["a_sev"], b["a_sev"] + te["sev"] + 1)
                        r = te["jc"] - 4 * i
                        ins = v.tensor_tensor(ssq[:, te["sev"] % 3, :],
                                              ssq[:, te["sev"] % 3, :],
                                              P["maskb"][:, r, :], ALU.mult)
                        S.inc("a_dg", ins, 1)
                v.wait_ge(S.h["a_omm"], b["a_omm"] + ent["om0"] + ent["njc"])
                v.tensor_single_scalar(zrec[0:1, :], ps_o[64:65, :], LN_EPS, ALU.add)
                ins = v.reciprocal(zrec[0:1, :], zrec[0:1, :])
                S.inc("a_zr", ins, 1)
                v.wait_ge(S.h["a_zk"], b["a_zk"] + ent["ei"] + 1)
                ins = v.tensor_copy(zb[:, :], pz[0:64, :])
                S.inc("a_zbev", ins, 1)
                tn2 = h // 2
                ins = v.tensor_tensor(o_i[64 * (h % 2):64 * (h % 2) + 64, tn2, :],
                                      ps_o[0:64, :], zb[:, :], ALU.mult)
                S.inc("a_oev", ins, 1)
                if h == 15:
                    if i >= 1:
                        v.wait_ge(S.h["eps"], b["eps"] + 16 * i)
                    for m in range(8):
                        gm = 8 * i + m
                        v.wait_ge(S.h["a_wo"], b["a_wo"] + gm + 1)
                        v.wait_ge(S.h["epx"], b["epx"] + 16 * (i + 1))
                        ins = v.scalar_tensor_tensor(outst[:, m, :], py[m % 2][:, :], 1.0,
                                                     xf[:, m, :], ALU.mult, ALU.add)
                        S.inc("epv", ins, 1)

        @blk.sync
        def _(sy):
            for i in range(8):
                if i >= 1:
                    sy.wait_ge(S.h["epv"], b["epv"] + 8 * i)
                S.inc("epx", sy.dma_start(out=xf[:, :, :],
                                          in_=xslice_ap(xin_dram, i * 512, 512)), 16)
                sy.wait_ge(S.h["epv"], b["epv"] + 8 * (i + 1))
                S.inc("eps", sy.dma_start(
                    out=xslice_ap(xout_dram, i * 512, 512), in_=outst[:, :, :]), 16)

      if dbg is not None:
        _dbg_dump(nc, S, o_i, dbg["oi"])


def omm_of_sev(ents, sev_idx):
    for ent in ents:
        for te in ent["tiles"]:
            if te["sev"] == sev_idx:
                return te["omm"]
    raise KeyError(sev_idx)


def emit_attn0_stage(nc, S, Dh, xin, xout):
    with ExitStack() as st:
        sb = lambda n, sh, dt: st.enter_context(nc.sbuf_tensor(_u(n), sh, dt))
        P = dict(
            ones1=sb("ones1", [128, 128], F32),
            idb=sb("idb", [128, 128], BF16),
            gq_bc=sb("gq_bc", [128, 256], F32),
            gk_bc=sb("gk_bc", [128, 256], F32),
            bq_bc=sb("bq_bc", [128, 256], F32),
            bk_bc=sb("bk_bc", [128, 256], F32),
            v65=sb("v65", [128, 32, 16, 65], BF16),
            maskb=sb("maskb", [128, 4, 512], BF16),
            mhalf=sb("mhalf", [32, 2, 128], BF16),
        )
        stQ = ExitStack()
        P["qtok"] = stQ.enter_context(nc.sbuf_tensor(_u("qtok"), [128, 32, 256], BF16, side="right"))
        P["ktok"] = stQ.enter_context(nc.sbuf_tensor(_u("ktok"), [128, 32, 256], BF16, side="right"))
        with nc.Block() as blk:
            @blk.gpsimd
            def _(g):
                g.memset(P["ones1"][:, :], 1.0)
                g.memset(P["v65"][:, :, :, :], 1.0)
                S.inc("dg", g.dma_start(out=P["idb"][:, :], in_=Dh["ident"].ap()), 16)
                S.inc("dg", g.dma_start(out=P["gk_bc"][:, :], in_=bass.AP(Dh["gb"], 0, [[256, 128], [1, 256]])), 16)
                S.inc("dg", g.dma_start(out=P["bk_bc"][:, :], in_=bass.AP(Dh["gb"], 128 * 256, [[256, 128], [1, 256]])), 16)
                S.inc("dg", g.dma_start(out=P["maskb"][:, :, :], in_=Dh["masks"].ap()), 16)
                S.inc("dg", g.dma_start(out=P["mhalf"][:, :, :], in_=Dh["mhalf"].ap()), 16)
                g.wait_ge(S.h["dg"], S.c["dg"])

            @blk.vector
            def _(v):
                v.wait_ge(S.h["dg"], S.c["dg"])
                v.tensor_single_scalar(P["gq_bc"][:, :], P["gk_bc"][:, :], 0.25, ALU.mult)
                v.tensor_single_scalar(P["bq_bc"][:, :], P["bk_bc"][:, :], 0.25, ALU.mult)

        with ExitStack() as stB:
            sb2 = lambda n, sh, dt: stB.enter_context(nc.sbuf_tensor(_u(n), sh, dt))
            xn = sb2("a_xn", [128, 8, 2048], BF16)
            P["wq_sb"] = sb2("wq_sb", [128, 8, 256], BF16)
            P["wk_sb"] = sb2("wk_sb", [128, 8, 256], BF16)
            P["wv_sb"] = sb2("wv_sb", [128, 8, 1024], BF16)
            with nc.Block() as blk:
                @blk.gpsimd
                def _(g):
                    S.inc("dg", g.dma_start(out=P["wq_sb"][:, :, :],
                                            in_=ap3(Dh["WqT"], 0, (256, 128), (128 * 256, 8), (1, 256))), 16)
                    S.inc("dg", g.dma_start(out=P["wk_sb"][:, :, :],
                                            in_=ap3(Dh["WkT"], 0, (256, 128), (128 * 256, 8), (1, 256))), 16)
                    S.inc("dg", g.dma_start(out=P["wv_sb"][:, :, :],
                                            in_=ap3(Dh["WvT"], 0, (D, 128), (128 * D, 8), (1, D))), 16)
                    g.wait_ge(S.h["dg"], S.c["dg"])
            for m in range(2):
                emit_ln(nc, S, {"ones1": P["ones1"]}, xin, xn, m, 4)
                emit_attn0_qkv(nc, S, P, xn, c0=16 * m, nc_chunks=16)

        P["qfT"] = [st.enter_context(nc.sbuf_tensor(_u("qfT"), [128, 2, 4096], BF16)) for _ in range(2)]
        P["kfT"] = [st.enter_context(nc.sbuf_tensor(_u("kfT"), [128, 4096], BF16)) for _ in range(2)]
        emit_attn0_tr(nc, S, P)
        stQ.close()
        P["wo_sb"] = st.enter_context(nc.sbuf_tensor(_u("wo_sb"), [128, 8, 1024], BF16))
        with nc.Block() as blk:
            @blk.gpsimd
            def _(g):
                S.inc("dg", g.dma_start(out=P["wo_sb"][:, :, :],
                                        in_=ap3(Dh["WoT"], 0, (D, 128), (128 * D, 8), (1, D))), 16)
                g.wait_ge(S.h["dg"], S.c["dg"])
        emit_attn0_core(nc, S, P, xin, xout)


def build_attn0_nc(debug=False):
    nc = bass.Bass("TRN2", target_bir_lowering=False, detect_race_conditions=False)
    Dh = dict(
        ident=nc.dram_tensor("ident", [128, 128], F32, kind="ExternalInput"),
        WqT=nc.dram_tensor("WqT", [D, 256], F32, kind="ExternalInput"),
        WkT=nc.dram_tensor("WkT", [D, 256], F32, kind="ExternalInput"),
        WvT=nc.dram_tensor("WvT", [D, D], F32, kind="ExternalInput"),
        WoT=nc.dram_tensor("WoT", [D, D], F32, kind="ExternalInput"),
        gb=nc.dram_tensor("gb", [2, 128, 256], F32, kind="ExternalInput"),
        masks=nc.dram_tensor("masks", [128, 4, 512], F32, kind="ExternalInput"),
        mhalf=nc.dram_tensor("mhalf", [32, 2, 128], F32, kind="ExternalInput"),
    )
    xin = nc.dram_tensor("xin", [D, T], F32, kind="ExternalInput")
    xout = nc.dram_tensor("xout", [D, T], F32, kind="ExternalOutput")
    with ExitStack() as stS:
        S = Sems(nc, stS, ALL_SEMS)
        emit_attn0_stage(nc, S, Dh, xin, xout)
        with nc.Block() as blk:
            @blk.sync
            def _(sy):
                sy.wait_ge(S.h["eps"], S.c["eps"])
    return nc



ATTN1_SEMS = ["sfl", "b_mm", "b_ev", "b_tr", "b_kst", "b_trc", "b_vst", "b_kv",
              "b_kdma", "b_sq", "b_mh", "b_pm", "b_smm", "b_sev", "b_dg",
              "b_omm", "b_oacc", "b_zr", "b_zk", "b_zbev", "b_wo", "b_qtr", "b_qn", "b_qns",
              "b_qtrc", "b_kld", "b_kvdone", "b_pre", "b_prv", "b_kvd2", "b_mtr", "b_mtrc", "b_mtr2"]

ALL_SEMS = list(dict.fromkeys(FFN_SEMS + ATTN0_SEMS + ATTN1_SEMS))



def emit_attn1_qkv(nc, S, P, h, kT_sc, v65_sc, c0, nchunks):
    b = S.base("b_mm", "b_ev", "b_tr", "b_kst", "b_trc", "b_vst", "b_kv",
               "b_kdma", "b_sq", "b_qn", "b_qns")
    with ExitStack() as ps:
      with nc.Block() as blk:
        sb = lambda n, sh, dt: ps.enter_context(nc.sbuf_tensor(_u(n), sh, dt))
        sc1 = sb("b_sc1", [128, 16, 32], F32)
        sc2 = sb("b_sc2", [128, 16, 32], F32)

        ktmp = sb("b_ktmp", [128, 1024], BF16)
        kstage = sb("b_kstage", [128, 8, 128], BF16)
        vstage = sb("b_vstage", [128, 16, 65], BF16)
        pk = ps.enter_context(nc.psum_tensor(_u("b_pqkv"), [128, 6, 512], F32))
        ptr = [ps.enter_context(nc.psum_tensor(_u("b_ptr"), [128, 128], BF16))
               for _ in range(2)]

        @blk.gpsimd
        def _(g):
            g.memset(vstage[:, :, :], 1.0)

        @blk.tensor
        def _(t):
            for cc in range(nchunks):
                if cc >= 1:
                    t.wait_ge(S.h["b_vst"], b["b_vst"] + cc)
                ins = None
                for K in range(8):
                    for n6 in range(6):
                        ins = t.matmul(pk[:, n6, :], h[:, K, cc * 128:(cc + 1) * 128],
                                       P["wqkv_sb"][:, K, n6 * 512:(n6 + 1) * 512],
                                       start=(K == 0), stop=(K == 7))
                S.inc("b_mm", ins, 1)
                t.wait_ge(S.h["b_tr"], b["b_tr"] + cc + 1)
                for g2 in range(8):
                    if cc * 8 + g2 >= 2:
                        t.wait_ge(S.h["b_trc"], b["b_trc"] + cc * 8 + g2 - 1)
                    ins = t.transpose(ptr[g2 % 2][:, :], ktmp[:, g2 * 128:(g2 + 1) * 128],
                                      P["idb"][:, :])
                    S.inc("b_kst", ins, 1)

        @blk.vector
        def _(v):
            FL = lambda ins: S.flush(v, "vfl", ins)
            for cc in range(nchunks):
                c = c0 + cc
                v.wait_ge(S.h["b_mm"], b["b_mm"] + cc + 1)
                cosb = P["cos_sb"][:, c, :].rearrange("p (h u) -> p h u", h=1)\
                    .broadcast_to((128, 16, 32))
                sinb = P["sin_sb"][:, c, :].rearrange("p (h u) -> p h u", h=1)\
                    .broadcast_to((128, 16, 32))
                for sec in range(2):
                    A = pk[:, 2 * sec:2 * sec + 2, :].rearrange(
                        "p a (h2 u two) -> p (a h2) u two", h2=8, u=32, two=2)
                    E = A[:, :, :, 0]
                    O = A[:, :, :, 1]
                    dst = (P["q_tok"][:, c, :] if sec == 0 else ktmp[:, :])
                    DD = dst.rearrange("p (h u two) -> p h u two", h=16, u=32, two=2)
                    v.tensor_tensor(sc1[:, :, :], E, cosb, ALU.mult)
                    v.tensor_tensor(sc2[:, :, :], O, sinb, ALU.mult)
                    v.tensor_tensor(DD[:, :, :, 0], sc1[:, :, :], sc2[:, :, :], ALU.subtract)
                    v.tensor_tensor(sc1[:, :, :], O, cosb, ALU.mult)
                    v.tensor_tensor(sc2[:, :, :], E, sinb, ALU.mult)
                    ins = v.tensor_tensor(DD[:, :, :, 1], sc1[:, :, :], sc2[:, :, :], ALU.add)
                    if sec == 1:
                        S.inc("b_tr", ins, 1)
                if cc >= 1:
                    v.wait_ge(S.h["b_kdma"], b["b_kdma"] + 32 * cc)
                ins = v.tensor_copy(vstage[:, :, 0:64],
                                    pk[:, 4:6, :].rearrange("p a (h2 f) -> p (a h2) f", h2=8))
                S.inc("b_vst", ins, 1)
                for g2 in range(8):
                    v.wait_ge(S.h["b_kst"], b["b_kst"] + cc * 8 + g2 + 1)
                    ins = v.tensor_copy(kstage[:, g2, :], ptr[g2 % 2][:, :])
                    S.inc("b_trc", ins, 1)
                S.inc("b_kv", v.nop(), 1)

        @blk.sync
        def _(sy):
            for cc in range(nchunks):
                c = c0 + cc
                sy.wait_ge(S.h["b_kv"], b["b_kv"] + cc + 1)
                S.inc("b_kdma", sy.dma_start(
                    out=ap3(kT_sc, c * 128, (T, 128), (128 * T, 8), (1, 128)),
                    in_=kstage[:, :, :]), 16)
                sy.wait_ge(S.h["b_vst"], b["b_vst"] + cc + 1)
                S.inc("b_kdma", sy.dma_start(
                    out=bass.AP(v65_sc, c * 128 * 1040, [[1040, 128], [1, 1040]]),
                    in_=vstage[:, :, :].rearrange("p h f -> p (h f)")), 16)


def emit_attn1_core(nc, S, P, xin_dram, xout_dram, kT_sc, v65_sc, dbg=None):
    for i in range(8):
        _attn1_core_i(nc, S, P, xin_dram, xout_dram, kT_sc, v65_sc, i, dbg=dbg)


def _attn1_core_i(nc, S, P, xin_dram, xout_dram, kT_sc, v65_sc, i, dbg=None):
    bA = S.base("b_qtr", "b_qtrc")
    with ExitStack() as ps:
      with nc.Block() as blk:
        pqt = [ps.enter_context(nc.psum_tensor(_u("b_pqt"), [128, 128], BF16))
               for _ in range(2)]

        @blk.tensor
        def _(t):
            idx = 0
            for ci in range(4):
                for bb in range(8):
                    if idx >= 2:
                        t.wait_ge(S.h["b_qtrc"], bA["b_qtrc"] + idx - 1)
                    ins = t.transpose(pqt[idx % 2][:, :],
                                      P["q_tok"][:, 4 * i + ci, 128 * bb:128 * (bb + 1)],
                                      P["idb"][:, :])
                    S.inc("b_qtr", ins, 1)
                    idx += 1

        @blk.vector
        def _(v):
            idx = 0
            for ci in range(4):
                for bb in range(8):
                    v.wait_ge(S.h["b_qtr"], bA["b_qtr"] + idx + 1)
                    ins = v.tensor_copy(P["qT"][:, bb, ci * 128:(ci + 1) * 128],
                                        pqt[idx % 2][:, :])
                    S.inc("b_qtrc", ins, 1)
                    idx += 1

    bP = S.base("b_pre", "b_prv", "b_kld", "b_kvd2", "b_mtr", "b_mtrc", "b_mtr2", "b_kdma")
    with ExitStack() as ps:
      with nc.Block() as blk:
        sb = lambda n, sh, dt: ps.enter_context(nc.sbuf_tensor(_u(n), sh, dt))
        ng = i + 2
        rmxall = sb("b_rmxall", [128, 4, 16, ng], F32)
        rmxs = sb("b_rmxs", [128, 4, 16], F32)
        pps = [ps.enter_context(nc.psum_tensor(_u("b_pps"), [128, 512], F32))
               for _ in range(3)]
        pmt = [ps.enter_context(nc.psum_tensor(_u("b_pmt"), [16, 128], F32))
               for _ in range(2)]

        @blk.sync
        def _(sy):
            if i == 0:
                sy.wait_ge(S.h["b_kdma"], bP["b_kdma"])
            for g in range(i + 1):
                if g >= 2:
                    sy.wait_ge(S.h["b_kvd2"], bP["b_kvd2"] + g - 1)
                S.inc("b_kld", sy.dma_start(
                    out=P["kbuf"][g % 2][:, :, :],
                    in_=ap3(kT_sc, g * 512, (T, 128), (128 * T, 8), (1, 512))), 16)

        @blk.tensor
        def _(t):
            idx = 0
            for gi2 in range(i + 2):
                if gi2 >= 1:
                    t.wait_ge(S.h["b_kld"], bP["b_kld"] + 16 * gi2)
                ins = None
                for ci in range(4):
                    for h in range(16):
                        base = 64 * (h % 2)
                        if idx >= 3:
                            t.wait_ge(S.h["b_prv"], bP["b_prv"] + idx - 2)
                        lh = P["qT"][base:base + 64, h // 2, ci * 128:(ci + 1) * 128]
                        if gi2 == 0:
                            ins = t.matmul(pps[idx % 3][:, 0:4], lh,
                                           P["pmT_sb"][base:base + 64, h // 2, :],
                                           start=True, stop=True,
                                           tile_position=(base, 0))
                        else:
                            ins = t.matmul(pps[idx % 3][:, :], lh,
                                           P["kbuf"][(gi2 - 1) % 2][base:base + 64,
                                                                    h // 2, :],
                                           start=True, stop=True,
                                           tile_position=(base, 0))
                        S.inc("b_pre", ins, 1)
                        idx += 1
                if gi2 >= 1:
                    S.inc("b_kvd2", t.nop(), 1)
            for ci in range(4):
                t.wait_ge(S.h["b_mtr"], bP["b_mtr"] + ci + 1)
                if bP["b_mtrc"] + ci >= 2:
                    t.wait_ge(S.h["b_mtrc"], bP["b_mtrc"] + ci - 1)
                ins = t.transpose(pmt[ci % 2][:, :], rmxs[:, ci, :], P["idf"][:, :])
                S.inc("b_mtr2", ins, 1)

        @blk.vector
        def _(v):
            FL = lambda ins: S.flush(v, "vfl", ins)
            idx = 0
            for gi2 in range(i + 2):
                for ci in range(4):
                    for h in range(16):
                        v.wait_ge(S.h["b_pre"], bP["b_pre"] + idx + 1)
                        src = pps[idx % 3][:, 0:4] if gi2 == 0 else pps[idx % 3][:, :]
                        ins = v.reduce_max(rmxall[:, ci, h, gi2:gi2 + 1], src, AX.X)
                        S.inc("b_prv", ins, 1)
                        idx += 1
            for ci in range(4):
                FL(v.tensor_reduce(rmxs[:, ci, :], rmxall[:, ci, :, :], AX.X, ALU.max))
                ins = v.tensor_single_scalar(rmxs[:, ci, :], rmxs[:, ci, :], -1.0,
                                             ALU.mult)
                S.xinc(v, "vfl", ins, "b_mtr", 1)
                v.wait_ge(S.h["b_mtr2"], bP.get("b_mtr2", S.c["b_mtr2"] - 4) + ci + 1) if False else None
            for ci in range(4):
                v.wait_ge(S.h["b_mtr2"], bP["b_mtr2"] + ci + 1)
                ins = v.tensor_copy(P["mT"][0:16, ci * 128:(ci + 1) * 128],
                                    pmt[ci % 2][0:16, :])
                S.inc("b_mtrc", ins, 1)

    glist = ["pm"] + list(range(i + 1))
    b = S.base("b_smm", "b_sev", "b_dg", "b_omm", "b_oacc", "b_zr", "b_zk",
               "b_zbev", "b_kld", "b_kvdone", "b_kdma")
    with ExitStack() as ps:
      with nc.Block() as blk:
        sb = lambda n, sh, dt: ps.enter_context(nc.sbuf_tensor(_u(n), sh, dt))
        ee = sb("b_e", [128, 3, 512], BF16)
        epm = sb("b_epm", [4, 512], BF16)
        P["_ee"] = ee
        P["_epm"] = epm
        zrec = sb("b_zrec", [1, 512], F32)
        zb = sb("b_zb", [64, 512], F32)
        ps_s = [ps.enter_context(nc.psum_tensor(_u("b_ps"), [128, 512], F32))
                for _ in range(3)]
        ps_pm = ps.enter_context(nc.psum_tensor(_u("b_ppm"), [4, 512], F32))
        po = [ps.enter_context(nc.psum_tensor(_u("b_po"), [128, 512], F32))
              for _ in range(2)]
        pz = ps.enter_context(nc.psum_tensor(_u("b_pz"), [64, 512], F32))

        nsm = [0]; nse = [0]; ndg = [0]; nom = [0]; nacc = [0]
        sched = []
        for gi, g in enumerate(glist):
            for h in range(16):
                ent = dict(g=g, gi=gi, h=h, sm0=nsm[0], se0=nse[0], dg0=ndg[0],
                           om0=nom[0], ac0=nacc[0])
                if g == "pm":
                    nsm[0] += 1; nse[0] += 1; nom[0] += 1
                else:
                    nsm[0] += 4; nse[0] += 4; nom[0] += 4
                    if g == i:
                        ndg[0] += 4
                nacc[0] += 1
                sched.append(ent)

        @blk.sync
        def _(sy):
            if i == 0:
                sy.wait_ge(S.h["b_kdma"], b["b_kdma"])
            for gi, g in enumerate(glist):
                if g == "pm":
                    continue
                gg = gi - 1
                if gg >= 2:
                    sy.wait_ge(S.h["b_kvdone"], b["b_kvdone"] + gg - 1)
                S.inc("b_kld", sy.dma_start(
                    out=P["kbuf"][gg % 2][:, :, :],
                    in_=ap3(kT_sc, g * 512, (T, 128), (128 * T, 8), (1, 512))), 16)
                S.inc("b_kld", sy.dma_start(
                    out=P["vbuf"][gg % 2][:, :, :, :].rearrange("p q h f -> p q (h f)"),
                    in_=ap3(v65_sc, g * 512 * 1040, (1040, 128), (128 * 1040, 4), (1, 1040))
                    ), 16)

        @blk.tensor
        def _(t):
            for ent in sched:
                g, gi, h = ent["g"], ent["gi"], ent["h"]
                if g == "pm":
                    if ent["sm0"] >= 1:
                        t.wait_ge(S.h["b_sev"], b["b_sev"] + ent["se0"])
                    base = 64 * (h % 2)
                    ins = t.matmul(ps_pm[:, :], P["pmT_sb"][base:base + 64, h // 2, :],
                                   P["qT"][base:base + 64, h // 2, :],
                                   start=True, stop=False, tile_position=(base, 0))
                    ins = t.matmul(ps_pm[:, :], P["sel16_sb"][0:16, h, 0:4],
                                   P["mT"][0:16, :], start=False, stop=True)
                    S.inc("b_smm", ins, 1)
                    t.wait_ge(S.h["b_sev"], b["b_sev"] + ent["se0"] + 1)
                    if ent["ac0"] >= 1:
                        t.wait_ge(S.h["b_oacc"], b["b_oacc"] + ent["ac0"] - 1)
                    ins = t.matmul(po[ent["ac0"] % 2][0:65, :],
                                   P["pmv65_sb"][0:4, h, 0:65],
                                   epm[0:4, :], start=True, stop=True)
                    S.inc("b_omm", ins, 1)
                else:
                    gg = gi - 1
                    if h == 0:
                        t.wait_ge(S.h["b_kld"], b["b_kld"] + 32 * (gg + 1))
                    base = 64 * (h % 2)
                    for jc in range(4):
                        sm = ent["sm0"] + jc
                        if sm >= 3:
                            t.wait_ge(S.h["b_sev"], b["b_sev"] + sm - 2)
                        t.matmul(ps_s[sm % 3][:, :],
                                 P["kbuf"][gg % 2][base:base + 64, h // 2,
                                                   jc * 128:(jc + 1) * 128],
                                 P["qT"][base:base + 64, h // 2, :],
                                 start=True, stop=False,
                                 tile_position=(base, 0))
                        ins = t.matmul(ps_s[sm % 3][:, :], P["sel16_sb"][0:16, h, :],
                                       P["mT"][0:16, :], start=False, stop=True)
                        S.inc("b_smm", ins, 1)
                    if ent["ac0"] >= 1:
                        t.wait_ge(S.h["b_oacc"], b["b_oacc"] + ent["ac0"] - 1)
                    for jc in range(4):
                        se = ent["se0"] + jc
                        if g == i:
                            t.wait_ge(S.h["b_dg"], b["b_dg"] + ent["dg0"] + jc + 1)
                        else:
                            t.wait_ge(S.h["b_sev"], b["b_sev"] + se + 1)
                        ins = t.matmul(po[ent["ac0"] % 2][0:65, :],
                                       P["vbuf"][gg % 2][:, jc, h, 0:65],
                                       ee[:, se % 3, :],
                                       start=(jc == 0), stop=(jc == 3))
                        S.inc("b_omm", ins, 1)
                    if h == 15:
                        S.inc("b_kvdone", t.nop(), 1)
                if ent["h"] == 15 and gi == len(glist) - 1:
                    for hh in range(16):
                        t.wait_ge(S.h["b_zr"], b["b_zr"] + hh + 1)
                        if b["b_zbev"] + hh >= 1:
                            t.wait_ge(S.h["b_zbev"], b["b_zbev"] + hh)
                        ins = t.matmul(pz[:, :], P["ones1"][0:1, 0:64], zrec[0:1, :],
                                       start=True, stop=True)
                        S.inc("b_zk", ins, 1)

        @blk.scalar
        def _(s):
            for ent in sched:
                g, h = ent["g"], ent["h"]
                if g == "pm":
                    s.wait_ge(S.h["b_smm"], b["b_smm"] + ent["sm0"] + 1)
                    ins = s.activation(out=epm[0:4, :], in_=ps_pm[0:4, :], func=AF.Exp,
                                       scale=0.125)
                    S.xinc(s, "sfl", ins, "b_sev", 1)
                else:
                    for jc in range(4):
                        sm = ent["sm0"] + jc
                        se = ent["se0"] + jc
                        s.wait_ge(S.h["b_smm"], b["b_smm"] + sm + 1)
                        if se >= 3:
                            s.wait_ge(S.h["b_omm"], b["b_omm"] + omm_of_se(sched, se - 3) + 1)
                        ins = s.activation(out=ee[:, se % 3, :], in_=ps_s[sm % 3][:, :],
                                           func=AF.Exp, scale=0.125)
                        S.xinc(s, "sfl", ins, "b_sev", 1)

        @blk.vector
        def _(v):
            FL = lambda ins: S.flush(v, "vfl", ins)
            for ent in sched:
                g, gi, h = ent["g"], ent["gi"], ent["h"]
                if g == i and g != "pm":
                    for jc in range(4):
                        se = ent["se0"] + jc
                        v.wait_ge(S.h["b_sev"], b["b_sev"] + se + 1)
                        ins = v.tensor_tensor(ee[:, se % 3, :], ee[:, se % 3, :],
                                              P["maskb"][:, jc, :], ALU.mult)
                        S.inc("b_dg", ins, 1)
                v.wait_ge(S.h["b_omm"], b["b_omm"] + ent["om0"] +
                          (1 if g == "pm" else 4))
                dst = P["oacc"][0:65, h, :]
                if g == "pm":
                    ins = v.tensor_copy(dst, po[ent["ac0"] % 2][0:65, :])
                else:
                    ins = v.scalar_tensor_tensor(dst, po[ent["ac0"] % 2][0:65, :], 1.0,
                                                 dst, ALU.mult, ALU.add)
                S.inc("b_oacc", ins, 1)
            for hh in range(16):
                v.tensor_single_scalar(zrec[0:1, :], P["oacc"][64:65, hh, :],
                                       1e-30, ALU.max)
                ins = v.reciprocal(zrec[0:1, :], zrec[0:1, :])
                S.xinc(v, "vfl", ins, "b_zr", 1)
                v.wait_ge(S.h["b_zk"], b["b_zk"] + hh + 1)
                ins = v.tensor_copy(zb[:, :], pz[0:64, :])
                S.inc("b_zbev", ins, 1)
                v.tensor_tensor(P["o_i"][64 * (hh % 2):64 * (hh % 2) + 64, hh // 2, :],
                                P["oacc"][0:64, hh, :], zb[:, :], ALU.mult)

    if dbg is not None and i == 0:
        with nc.Block() as blkD:
            @blkD.sync
            def _(sy):
                S.inc("eps", sy.dma_start(out=dbg["oacc"].ap()[0:65, :, :],
                                          in_=P["oacc"][0:65, :, :]), 16)
                sy.wait_ge(S.h["eps"], S.c["eps"])
        _dbg_dump(nc, S, P["qT"], dbg["qT"])
        _dbg_dump(nc, S, P["_ee"], dbg["ee"])
        _dbg_dump(nc, S, P["_epm"], dbg["epm"])
        _dbg_dump(nc, S, P["kbuf"][0], dbg["kbuf"])
        _dbg_dump(nc, S, P["vbuf"][0], dbg["vbuf"])
        _dbg_dump(nc, S, P["o_i"], dbg["o_i"])

    bC = S.base("b_wo", "epx", "epv", "eps")
    with ExitStack() as ps:
      with nc.Block() as blk:
        sb = lambda n, sh, dt: ps.enter_context(nc.sbuf_tensor(_u(n), sh, dt))
        xf = sb("b_xf", [128, 4, 512], F32)
        outst = sb("b_outst", [128, 4, 512], F32)
        py = [ps.enter_context(nc.psum_tensor(_u("b_py"), [128, 512], F32))
              for _ in range(2)]

        @blk.tensor
        def _(t):
            for m in range(8):
                if m >= 2:
                    t.wait_ge(S.h["epv"], bC["epv"] + m - 1)
                ins = None
                for oc in range(8):
                    ins = t.matmul(py[m % 2][:, :],
                                   P["wout_sb"][:, oc, m * 128:(m + 1) * 128],
                                   P["o_i"][:, oc, :], start=(oc == 0), stop=(oc == 7))
                S.inc("b_wo", ins, 1)

        @blk.vector
        def _(v):
            for half in range(2):
                if i + half >= 1:
                    v.wait_ge(S.h["eps"], bC["eps"] - 16 + 16 * half)
                for mm in range(4):
                    m = 4 * half + mm
                    v.wait_ge(S.h["b_wo"], bC["b_wo"] + m + 1)
                    v.wait_ge(S.h["epx"], bC["epx"] + 16 * (half + 1))
                    ins = v.scalar_tensor_tensor(outst[:, mm, :], py[m % 2][:, :], 1.0,
                                                 xf[:, mm, :], ALU.mult, ALU.add)
                    S.inc("epv", ins, 1)

        @blk.sync
        def _(sy):
            for half in range(2):
                if half >= 1:
                    sy.wait_ge(S.h["epv"], bC["epv"] + 4)
                S.inc("epx", sy.dma_start(
                    out=xf[:, :, :],
                    in_=ap3(xin_dram, i * 512 + half * 4 * 128 * T,
                            (T, 128), (128 * T, 4), (1, 512))), 16)
                sy.wait_ge(S.h["epv"], bC["epv"] + 4 * (half + 1))
                S.inc("eps", sy.dma_start(
                    out=ap3(xout_dram, i * 512 + half * 4 * 128 * T,
                            (T, 128), (128 * T, 4), (1, 512)),
                    in_=outst[:, :, :]), 16)


def omm_of_se(sched, se_idx):
    for ent in sched:
        n = 1 if ent["g"] == "pm" else 4
        if ent["se0"] <= se_idx < ent["se0"] + n:
            return ent["om0"] + (se_idx - ent["se0"])
    raise KeyError(se_idx)


def emit_attn1_stage(nc, S, Dh, xin, xout):
    kT_sc = nc.dram_tensor(_u("kT_sc"), [D, T], BF16, kind="Internal")
    v65_sc = nc.dram_tensor(_u("v65_sc"), [T, 1040], BF16, kind="Internal")
    with ExitStack() as st:
        sb = lambda n, sh, dt: st.enter_context(nc.sbuf_tensor(_u(n), sh, dt))
        P = dict(
            ones1=sb("ones1", [128, 128], F32),
            idf=sb("idf", [128, 128], F32),
            idb=sb("idb", [128, 128], BF16),
            maskb=sb("maskb", [128, 4, 512], BF16),
            pmv_sb=sb("pmv_sb", [16, 4, 64], F32),
            pmT_sb=sb("pmT_sb", [128, 8, 4], BF16),
            pmv65_sb=sb("pmv65", [4, 16, 65], BF16),
            q_tok=sb("q_tok", [128, 32, 1024], BF16),
            nwcols=[sb(f"nwcol", [128, 1], F32) for dc in range(8)],
        )
        with nc.Block() as blk:
            @blk.gpsimd
            def _(g):
                g.memset(P["ones1"][:, :], 1.0)
                g.memset(P["pmv65_sb"][:, :, :], 1.0)
                S.inc("dg", g.dma_start(out=P["idf"][:, :], in_=Dh["ident"].ap()), 16)
                S.inc("dg", g.dma_start(out=P["idb"][:, :], in_=Dh["ident"].ap()), 16)
                S.inc("dg", g.dma_start(out=P["maskb"][:, :, :], in_=Dh["masks"].ap()), 16)
                S.inc("dg", g.dma_start(out=P["pmv_sb"][:, :, :], in_=Dh["pm"].ap()), 16)
                S.inc("dg", g.dma_start(
                    out=P["pmT_sb"][0:64, :, :],
                    in_=ap3(Dh["pmT"], 0, (4, 64), (512, 8), (1, 4))), 16)
                S.inc("dg", g.dma_start(
                    out=P["pmT_sb"][64:128, :, :],
                    in_=ap3(Dh["pmT"], 256, (4, 64), (512, 8), (1, 4))), 16)
                S.inc("dg", g.dma_start(
                    out=P["pmv65_sb"][:, :, 0:64],
                    in_=ap3(Dh["pm"], 0, (64, 4), (256, 16), (1, 64))), 16)
                for dc in range(8):
                    S.inc("dg", g.dma_start(out=P["nwcols"][dc][:, :],
                                            in_=bass.AP(Dh["nw"], 128 * dc, [[1, 128], [0, 1]])), 16)
                g.wait_ge(S.h["dg"], S.c["dg"])

        with ExitStack() as stB:
            sb2 = lambda n, sh, dt: stB.enter_context(nc.sbuf_tensor(_u(n), sh, dt))
            hmac = sb2("b_h", [128, 8, 2048], BF16)
            P["wqkv_sb"] = sb2("wqkv_sb", [128, 8, 3072], BF16)
            P["cos_sb"] = sb2("cos_sb", [128, 32, 32], F32)
            P["sin_sb"] = sb2("sin_sb", [128, 32, 32], F32)
            with nc.Block() as blk:
                @blk.gpsimd
                def _(g):
                    S.inc("dg", g.dma_start(
                        out=P["wqkv_sb"][:, :, :],
                        in_=ap3(Dh["wqkvT"], 0, (3072, 128), (128 * 3072, 8), (1, 3072))), 16)
                    S.inc("dg", g.dma_start(out=P["cos_sb"][:, :, :], in_=Dh["cosr"].ap()), 16)
                    S.inc("dg", g.dma_start(out=P["sin_sb"][:, :, :], in_=Dh["sinr"].ap()), 16)
                    g.wait_ge(S.h["dg"], S.c["dg"])
            for m in range(2):
                emit_ln(nc, S, {"ones1": P["ones1"]}, xin, hmac, m, 4,
                        rms=dict(nw_cols=P["nwcols"]))
                emit_attn1_qkv(nc, S, P, hmac, kT_sc, v65_sc, 16 * m, 16)

        P["mT"] = sb("mT", [16, 512], BF16)
        P["sel16_sb"] = sb("sel16_sb", [16, 16, 128], BF16)
        with nc.Block() as blk:
            @blk.gpsimd
            def _(g):
                S.inc("dg", g.dma_start(out=P["sel16_sb"][:, :, :], in_=Dh["sel16"].ap()), 16)
                g.wait_ge(S.h["dg"], S.c["dg"])

        P["qT"] = sb("qT", [128, 8, 512], BF16)
        P["kbuf"] = [sb("kbuf", [128, 8, 512], BF16) for _ in range(2)]
        P["vbuf"] = [sb("vbuf", [128, 4, 16, 65], BF16) for _ in range(2)]
        P["oacc"] = sb("oacc", [128, 16, 512], F32)
        P["o_i"] = sb("o_i", [128, 8, 512], BF16)
        P["wout_sb"] = sb("wout_sb", [128, 8, 1024], BF16)
        with nc.Block() as blk:
            @blk.gpsimd
            def _(g):
                S.inc("dg", g.dma_start(
                    out=P["wout_sb"][:, :, :],
                    in_=ap3(Dh["woutT"], 0, (D, 128), (128 * D, 8), (1, D))), 16)
                g.wait_ge(S.h["dg"], S.c["dg"])
        emit_attn1_core(nc, S, P, xin, xout, kT_sc, v65_sc)


def build_attn1_nc(debug=False):
    nc = bass.Bass("TRN2", target_bir_lowering=False, detect_race_conditions=False)
    Dh = dict(
        ident=nc.dram_tensor("ident", [128, 128], F32, kind="ExternalInput"),
        wqkvT=nc.dram_tensor("wqkvT", [D, 3072], F32, kind="ExternalInput"),
        woutT=nc.dram_tensor("woutT", [D, D], F32, kind="ExternalInput"),
        nw=nc.dram_tensor("nw", [D], F32, kind="ExternalInput"),
        pm=nc.dram_tensor("pm", [16, 4, 64], F32, kind="ExternalInput"),
        pmT=nc.dram_tensor("pmT", [16, 64, 4], F32, kind="ExternalInput"),
        cosr=nc.dram_tensor("cosr", [128, 32, 32], F32, kind="ExternalInput"),
        sinr=nc.dram_tensor("sinr", [128, 32, 32], F32, kind="ExternalInput"),
        masks=nc.dram_tensor("masks", [128, 4, 512], F32, kind="ExternalInput"),
        sel16=nc.dram_tensor("sel16", [16, 16, 128], F32, kind="ExternalInput"),
    )
    xin = nc.dram_tensor("xin", [D, T], F32, kind="ExternalInput")
    xout = nc.dram_tensor("xout", [D, T], F32, kind="ExternalOutput")
    with ExitStack() as stS:
        S = Sems(nc, stS, ALL_SEMS)
        emit_attn1_stage(nc, S, Dh, xin, xout)
        with nc.Block() as blk:
            @blk.sync
            def _(sy):
                sy.wait_ge(S.h["eps"], S.c["eps"])
    return nc


def build_fused_nc():
    nc = bass.Bass("TRN2", target_bir_lowering=False, detect_race_conditions=False)
    Dh = dict(
        ident=nc.dram_tensor("ident", [128, 128], F32, kind="ExternalInput"),
        sel=nc.dram_tensor("sel", [8, 9, 128], F32, kind="ExternalInput"),
        masks=nc.dram_tensor("masks", [128, 4, 512], F32, kind="ExternalInput"),
        mhalf=nc.dram_tensor("mhalf", [32, 2, 128], F32, kind="ExternalInput"),
        gb=nc.dram_tensor("gb", [2, 128, 256], F32, kind="ExternalInput"),
        WqT=nc.dram_tensor("WqT", [D, 256], F32, kind="ExternalInput"),
        WkT=nc.dram_tensor("WkT", [D, 256], F32, kind="ExternalInput"),
        WvT=nc.dram_tensor("WvT", [D, D], F32, kind="ExternalInput"),
        WoT=nc.dram_tensor("WoT", [D, D], F32, kind="ExternalInput"),
        wqkvT=nc.dram_tensor("wqkvT", [D, 3072], F32, kind="ExternalInput"),
        woutT=nc.dram_tensor("woutT", [D, D], F32, kind="ExternalInput"),
        nw=nc.dram_tensor("nw", [D], F32, kind="ExternalInput"),
        pm=nc.dram_tensor("pm", [16, 4, 64], F32, kind="ExternalInput"),
        pmT=nc.dram_tensor("pmT", [16, 64, 4], F32, kind="ExternalInput"),
        cosr=nc.dram_tensor("cosr", [128, 32, 32], F32, kind="ExternalInput"),
        sinr=nc.dram_tensor("sinr", [128, 32, 32], F32, kind="ExternalInput"),
        sel16=nc.dram_tensor("sel16", [16, 16, 128], F32, kind="ExternalInput"),
    )
    Dh["fam_moe0"] = make_ffn_fam(nc, "moe0")
    Dh["fam_moe1"] = make_ffn_fam(nc, "moe1")
    xin = nc.dram_tensor("xin", [D, T], F32, kind="ExternalInput")
    xout = nc.dram_tensor("xout", [D, T], F32, kind="ExternalOutput")
    xr = [nc.dram_tensor(f"xr{i}", [D, T], F32, kind="Internal") for i in range(2)]

    with ExitStack() as stS:
        S = Sems(nc, stS, ALL_SEMS)
        chain = [xin, xr[0], xr[1], xr[0], xr[1], xr[0], xout]
        emit_ffn_stage(nc, S, Dh, "moe0", chain[0], chain[1])
        emit_attn0_stage(nc, S, Dh, chain[1], chain[2])
        emit_ffn_stage(nc, S, Dh, "moe0", chain[2], chain[3])
        emit_ffn_stage(nc, S, Dh, "moe1", chain[3], chain[4])
        emit_attn1_stage(nc, S, Dh, chain[4], chain[5])
        emit_ffn_stage(nc, S, Dh, "moe1", chain[5], chain[6])
        with nc.Block() as blk:
            @blk.sync
            def _(sy):
                sy.wait_ge(S.h["eps"], S.c["eps"])
    return nc



_CACHE = {}


def _consts():
    ident = np.eye(128, dtype=np.float32)
    sel = np.zeros((8, 9, 128), dtype=np.float32)
    for u in range(8):
        sel[u, u, :] = 1.0
    jl = np.arange(128)[:, None, None]
    r4 = np.arange(4)[None, :, None]
    il = np.arange(512)[None, None, :]
    masks = np.ascontiguousarray(((128 * r4 + jl) <= il).astype(np.float32))
    mh = np.zeros((32, 2, 128), np.float32)
    mh[0:16, 0, :] = 1.0
    mh[16:32, 1, :] = 1.0
    invf = 1.0 / (10000.0 ** (np.arange(0, 64, 2, dtype=np.float64) / 64))
    tpos = (np.arange(T).reshape(32, 128).T)[:, :, None] * invf[None, None, :]
    cosr = np.ascontiguousarray(np.cos(tpos).astype(np.float32))
    sinr = np.ascontiguousarray(np.sin(tpos).astype(np.float32))
    sel16 = np.zeros((16, 16, 128), np.float32)
    for hh in range(16):
        sel16[hh, hh, :] = 1.0
    return dict(ident=ident, sel=sel, masks=masks, mhalf=mh, cosr=cosr,
                sinr=sinr, sel16=sel16)


def _anp(a):
    return np.ascontiguousarray(np.asarray(a, np.float32))


def kernel(**inputs):
    C = _consts()
    if "ncs" not in _CACHE:
        _CACHE["ncs"] = dict(
            moe0=build_ffn_nc("moe0"), attn0=build_attn0_nc(),
            moe1=build_ffn_nc("moe1"), attn1=build_attn1_nc(),
        )
    ncs = _CACHE["ncs"]

    m_moe0 = dict(
        ident=C["ident"], sel=C["sel"],
        Wg=_anp(inputs["L0_ffn_Wg"]), Ws1=_anp(inputs["L0_ffn_Ws1"]),
        Ws3=_anp(inputs["L0_ffn_Ws3"]), Ws2=_anp(inputs["L0_ffn_Ws2"]),
        We1=_anp(inputs["L0_ffn_We1"]), We3=_anp(inputs["L0_ffn_We3"]),
        We2=_anp(inputs["L0_ffn_We2"]))
    gamma = _anp(inputs["L0_attn_gamma"])
    beta = _anp(inputs["L0_attn_beta"])
    gb = np.ascontiguousarray(np.stack([
        np.broadcast_to(np.tile(gamma, 16), (128, 256)),
        np.broadcast_to(np.tile(beta, 16), (128, 256))]).astype(np.float32))
    m_attn0 = dict(
        ident=C["ident"], gb=gb, masks=C["masks"], mhalf=C["mhalf"],
        WqT=_anp(np.asarray(inputs["L0_attn_Wq"]).T),
        WkT=_anp(np.asarray(inputs["L0_attn_Wk"]).T),
        WvT=_anp(np.asarray(inputs["L0_attn_Wv"]).T),
        WoT=_anp(np.asarray(inputs["L0_attn_Wo"]).T))
    m_moe1 = dict(
        ident=C["ident"], sel=C["sel"],
        wrT=_anp(np.asarray(inputs["L1_ffn_w_router"]).T),
        w1=_anp(inputs["L1_ffn_w1"]), w3=_anp(inputs["L1_ffn_w3"]),
        w2=_anp(inputs["L1_ffn_w2"]))
    pm = _anp(inputs["L1_attn_persist_mem"])
    m_attn1 = dict(
        ident=C["ident"], masks=C["masks"], cosr=C["cosr"], sinr=C["sinr"],
        sel16=C["sel16"],
        wqkvT=_anp(np.asarray(inputs["L1_attn_w_qkv"]).T),
        woutT=_anp(np.asarray(inputs["L1_attn_w_out"]).T),
        nw=_anp(inputs["L1_attn_norm_weight"]),
        pm=pm, pmT=np.ascontiguousarray(pm.transpose(0, 2, 1)))

    x = np.asarray(inputs["L0_attn_x"], np.float32)
    cur = [np.ascontiguousarray(x[c].T) for c in range(8)]

    for stage, wm in [("moe0", m_moe0), ("attn0", m_attn0), ("moe0", m_moe0),
                      ("moe1", m_moe1), ("attn1", m_attn1), ("moe1", m_moe1)]:
        in_maps = [dict(xin=cur[c], **wm) for c in range(8)]
        res = run_bass_kernel_spmd(ncs[stage], in_maps, core_ids=list(range(8)))
        cur = [np.ascontiguousarray(res.results[c]["xout"]) for c in range(8)]

    out = np.stack([cur[c].T for c in range(8)]).astype(np.float32)
    return out
